# version 27
# baseline (speedup 1.0000x reference)
import numpy as np
import concourse.bass as bass
import concourse.mybir as mybir
from concourse.bass_utils import run_bass_kernel_spmd
from concourse.alu_op_type import AluOpType as ALU
from contextlib import ExitStack

F32, BF16 = mybir.dt.float32, mybir.dt.bfloat16
AF = mybir.ActivationFunctionType
AXX = mybir.AxisListType.X

NCORES = 8
T = 1024
TT = 8
D = 2048
DC = 16
E = 8192
EC = 64
NB = 8
BEC = 8
NSUP = 16
SUPE = 512
SUPC = 4
NSLAB = 8
EPS = 1e-5


def build_core():
    nc = bass.Bass("TRN2", target_bir_lowering=False, debug=False)

    u_in = nc.dram_tensor("u", [T, D], F32, kind="ExternalInput").ap()
    wb_in = nc.dram_tensor("w_b_rank", [D, 64], F32, kind="ExternalInput").ap()
    at_in = nc.dram_tensor("a_t", [8, 2, 2], F32, kind="ExternalInput").ap()
    pt_in = nc.dram_tensor("pt_all", [8, 2, 256], F32, kind="ExternalInput").ap()
    wbsT_in = nc.dram_tensor("w_bs_t", [256, 64], F32, kind="ExternalInput").ap()
    wcr_in = nc.dram_tensor("w_c_rank", [256, 64], F32, kind="ExternalInput").ap()
    wcm_in = nc.dram_tensor("w_c_model", [64, D], F32, kind="ExternalInput").ap()
    dvec_in = nc.dram_tensor("d_vec", [128, DC], F32, kind="ExternalInput").ap()
    wu_in = nc.dram_tensor("w_up", [D, E], F32, kind="ExternalInput").ap()
    wg_in = nc.dram_tensor("w_gate", [D, E], F32, kind="ExternalInput").ap()
    wd_in = nc.dram_tensor("w_down", [E, D], F32, kind="ExternalInput").ap()
    ident_in = nc.dram_tensor("ident", [128, 128], F32, kind="ExternalInput").ap()
    out = nc.dram_tensor("out", [T, D], F32, kind="ExternalOutput").ap()

    u_bf = nc.dram_tensor("u_bf_scr", [T, D], BF16)
    s_sc = nc.dram_tensor("s_scr", [T, D], BF16)
    scr2_d = nc.dram_tensor("scr2_d_scr", [E, D], BF16)
    scrh = [nc.dram_tensor(f"scrh{m}", [128, 1], F32) for m in range(3)]
    scrh2 = [nc.dram_tensor(f"scrh2_{m}", [1, 3], F32) for m in range(3)]

    ops = {e: [] for e in ("sync", "scalar", "vector", "tensor", "gpsimd")}
    cnt = {}
    es = ExitStack()
    EN = es.enter_context
    semnames = (["tv", "vv", "av", "misc", "ubf", "ufm", "wbq", "ssc0", "ssc1",
                 "sfm", "out0", "fin0", "fin1", "fin2", "msc2", "msc3",
                 "dtw0", "dtw1"]
                + [f"{p}{i}" for p in ("ldd", "lgg", "lgd", "tdw", "dt",
                                       "trU", "trG", "qd", "qw", "s2g") for i in range(3)]
                + [f"{p}{i}" for p in ("ld", "tw") for i in range(3)]
                + [f"wd{i}" for i in range(4)])
    sems = {n: EN(nc.semaphore(n)) for n in semnames}

    def W(eng, *deps):
        for sname, v in deps:
            if v and v > 0:
                ops[eng].append(lambda e, s=sems[sname], v_=v: e.wait_ge(s, v_))

    def DMA(eng, qname, outap, inap, **kw):
        cnt[qname] = cnt.get(qname, 0) + 16
        ops[eng].append(lambda e, o=outap, i=inap, s=sems[qname], kw_=kw:
                        e.dma_start(out=o, in_=i, **kw_).then_inc(s, 16))
        return cnt[qname]

    def DMAT(eng, qname, outap, inap):
        cnt[qname] = cnt.get(qname, 0) + 16
        ops[eng].append(lambda e, o=outap, i=inap, s=sems[qname]:
                        e.dma_start_transpose(out=o, in_=i).then_inc(s, 16))
        return cnt[qname]

    def OP(eng, sname, fn):
        cnt[sname] = cnt.get(sname, 0) + 1
        ops[eng].append(lambda e, f=fn, s=sems[sname]: f(e).then_inc(s, 1))
        return cnt[sname]

    def OPq(eng, fn):
        ops[eng].append(lambda e, f=fn: f(e))

    s_fm = EN(nc.sbuf_tensor("s_fm", [128, DC * T], BF16))
    ident = EN(nc.sbuf_tensor("ident_s", [128, 128], F32))
    hc = EN(nc.sbuf_tensor("hc", [128, 12], F32))
    hrow = EN(nc.sbuf_tensor("hrow", [1, 16], F32))
    rowb = EN(nc.sbuf_tensor("rowb", [1, 3 * 128], F32))
    acc_u = EN(nc.sbuf_tensor("acc_u", [128, 2 * EC], F32))
    acc_g = EN(nc.sbuf_tensor("acc_g", [128, EC], F32))
    acc_d = EN(nc.sbuf_tensor("acc_d", [128, EC], F32))
    accr = EN(nc.sbuf_tensor("accr", [128, 4], F32))
    bnst = EN(nc.sbuf_tensor("bnst", [128, 2 * 24], F32))
    mv = EN(nc.sbuf_tensor("mv", [128, TT * 2], F32))
    rstd = EN(nc.sbuf_tensor("rstd", [128, TT], F32))
    sqv = EN(nc.sbuf_tensor("sqv", [128, TT], F32))
    wst = [EN(nc.sbuf_tensor(f"wst{i}", [128, 1024], F32)) for i in range(6)]
    wstd = [EN(nc.sbuf_tensor(f"wstd{i}", [128, 2048], F32)) for i in range(2)]
    terd_t = [EN(nc.sbuf_tensor(f"terd_t{i}", [128, 2048], BF16)) for i in range(2)]

    h_u, nh_u, nsu = hc[:, 0:1], hc[:, 1:2], hc[:, 2:3]
    h_g, nh_g, nsg = hc[:, 4:5], hc[:, 5:6], hc[:, 6:7]
    h_d, nh_d, nsd = hc[:, 8:9], hc[:, 9:10], hc[:, 10:11]

    v_ident = DMA("sync", "misc", ident[:], ident_in)

    p1 = ExitStack()
    P1 = p1.enter_context
    u_fm = P1(nc.sbuf_tensor("u_fm", [128, DC * T], BF16))
    wbq = P1(nc.sbuf_tensor("wbq", [128, DC * 64], BF16))
    gf = P1(nc.sbuf_tensor("gf", [64, D], BF16))
    dp = P1(nc.sbuf_tensor("dp", [128, DC], F32))
    dvc = P1(nc.sbuf_tensor("dvc", [128, DC], F32))
    rbf = P1(nc.sbuf_tensor("rbf", [64, T], BF16))
    wstd2x = P1(nc.sbuf_tensor("wstd2x", [128, 2048], F32))

    v_dvc = DMA("sync", "msc2", dvc[:], dvec_in)
    v_wb = DMA("gpsimd", "wbq", wbq[:].rearrange("p (c r) -> p c r", r=64),
               wb_in.rearrange("(c p) r -> p c r", p=128))
    W("vector", ("msc2", v_dvc))
    v_dp = OP("vector", "vv", lambda e: e.tensor_scalar(
        out=dp[:], in0=dvc[:], scalar1=1.0, scalar2=None, op0=ALU.add))
    ucst = P1(nc.sbuf_tensor("ucst", [128, 2048], F32))
    ucb = P1(nc.sbuf_tensor("ucb", [128, 2048], BF16))
    vcast = 0
    vwr = 0
    for j in range(TT):
        if vcast:
            W("scalar", ("vv", vcast))
        vu = DMA("scalar", "ubf", ucst[:], u_in[128 * j:128 * (j + 1), :])
        W("vector", ("ubf", vu), *( [("ufm", vwr)] if vwr else []))
        vcast = OP("vector", "vv", lambda e: e.tensor_copy(out=ucb[:], in_=ucst[:]))
        W("scalar", ("vv", vcast))
        vwr = DMA("scalar", "ufm", u_bf[128 * j:128 * (j + 1), :], ucb[:])
    ut_v = 0
    W("scalar", ("ufm", vwr))
    for c in range(DC):
        ut_v = DMAT("scalar", "ufm", u_fm[:, T * c:T * (c + 1)],
                    u_bf[0:T, 128 * c:128 * (c + 1)])

    fold = ExitStack()
    FE = fold.enter_context
    at_s = FE(nc.sbuf_tensor("at_s", [2, 16], F32))
    pt_s = FE(nc.sbuf_tensor("pt_s", [2, 8 * 256], F32))
    t1_s = FE(nc.sbuf_tensor("t1_s", [2, 256], F32))
    m_s = FE(nc.sbuf_tensor("m_s", [128, 512], F32))
    wbsT_s = FE(nc.sbuf_tensor("wbsT_s", [128, 2 * 64], F32))
    wcr_s = FE(nc.sbuf_tensor("wcr_s", [128, 2 * 64], F32))
    wcm_s = FE(nc.sbuf_tensor("wcm_s", [64, D], F32))
    g1_s = FE(nc.sbuf_tensor("g1_s", [64, 256], F32))
    g1t_s = FE(nc.sbuf_tensor("g1t_s", [128, 2 * 64], F32))
    g2t_s = FE(nc.sbuf_tensor("g2t_s", [64, 64], F32))
    psf = ExitStack()
    ps_f = psf.enter_context(nc.psum_tensor("ps_f", [128, 512], F32))
    ps_f2 = psf.enter_context(nc.psum_tensor("ps_f2", [64, 512], F32))

    DMA("sync", "msc3", at_s[:].rearrange("j (l i) -> j l i", i=2),
        at_in.rearrange("l j i -> j l i"))
    DMA("sync", "msc3", pt_s[:].rearrange("j (l n) -> j l n", n=256),
        pt_in.rearrange("l j n -> j l n"))
    DMA("sync", "msc3", wbsT_s[:].rearrange("p (k m) -> p k m", m=64),
        wbsT_in.rearrange("(k p) m -> p k m", p=128))
    DMA("sync", "msc3", wcr_s[:].rearrange("p (k m) -> p k m", m=64),
        wcr_in.rearrange("(k p) m -> p k m", p=128))
    ld_fold = DMA("sync", "msc3", wcm_s[:], wcm_in)

    W("tensor", ("msc3", ld_fold), ("misc", v_ident))
    v_m = 0
    for lvl in range(8):
        if v_m:
            W("tensor", ("vv", v_m))
        v_t1 = OP("tensor", "tv", lambda e, l=lvl: e.matmul(
            ps_f2[0:2, 0:256], at_s[:, 2 * l:2 * (l + 1)],
            pt_s[:, 256 * l:256 * (l + 1)], start=True, stop=True))
        W("vector", ("tv", v_t1), ("vv", v_m))
        v_t1e = OP("vector", "vv", lambda e: e.tensor_copy(
            out=t1_s[:], in_=ps_f2[0:2, 0:256]))
        W("tensor", ("vv", v_t1e))
        for ch in range(2):
            v_c = OP("tensor", "tv", lambda e, l=lvl, c=ch: e.matmul(
                ps_f[:, 256 * c:256 * (c + 1)],
                pt_s[:, 256 * l + 128 * c:256 * l + 128 * (c + 1)],
                t1_s[:], start=True, stop=True))
        W("vector", ("tv", v_c))
        if lvl == 0:
            v_m = OP("vector", "vv", lambda e: e.tensor_copy(out=m_s[:], in_=ps_f[:]))
        else:
            v_m = OP("vector", "vv", lambda e: e.tensor_tensor(
                out=m_s[:], in0=m_s[:], in1=ps_f[:], op=ALU.mult))
    W("tensor", ("vv", v_m))
    OPq("tensor", lambda e: e.matmul(
        ps_f2[0:64, 0:256], wbsT_s[:, 0:64], m_s[:, 0:256], start=True, stop=False))
    v_g1 = OP("tensor", "tv", lambda e: e.matmul(
        ps_f2[0:64, 0:256], wbsT_s[:, 64:128], m_s[:, 256:512], start=False, stop=True))
    W("vector", ("tv", v_g1))
    v_g1e = OP("vector", "vv", lambda e: e.tensor_copy(
        out=g1_s[:], in_=ps_f2[0:64, 0:256]))
    W("tensor", ("vv", v_g1e))
    for k in range(2):
        v_tp = OP("tensor", "tv", lambda e, kk=k: e.transpose(
            ps_f[:, 64 * kk:64 * (kk + 1)], g1_s[:, 128 * kk:128 * (kk + 1)],
            ident[0:64, 0:64]))
    W("vector", ("tv", v_tp))
    v_g1t = OP("vector", "vv", lambda e: e.tensor_copy(
        out=g1t_s[:], in_=ps_f[:, 0:128]))
    W("tensor", ("vv", v_g1t))
    OPq("tensor", lambda e: e.matmul(
        ps_f2[0:64, 0:64], wcr_s[:, 0:64], g1t_s[:, 0:64], start=True, stop=False))
    v_g2 = OP("tensor", "tv", lambda e: e.matmul(
        ps_f2[0:64, 0:64], wcr_s[:, 64:128], g1t_s[:, 64:128], start=False, stop=True))
    W("vector", ("tv", v_g2))
    v_g2e = OP("vector", "vv", lambda e: e.tensor_copy(
        out=g2t_s[:], in_=ps_f2[0:64, 0:64]))
    W("tensor", ("vv", v_g2e))
    v_gfe = 0
    for k in range(4):
        if v_gfe:
            W("tensor", ("vv", v_gfe))
        v_gf = OP("tensor", "tv", lambda e, kk=k: e.matmul(
            ps_f[0:64, 0:512], g2t_s[:], wcm_s[:, 512 * kk:512 * (kk + 1)],
            start=True, stop=True))
        W("vector", ("tv", v_gf))
        v_gfe = OP("vector", "vv", lambda e, kk=k: e.tensor_copy(
            out=gf[:, 512 * kk:512 * (kk + 1)], in_=ps_f[0:64, 0:512]))
    psf.close()
    fold.close()

    def emit_fin(m, vdep, acc, q_eng):
        fsem = f"fin{m}"
        W("vector", vdep if isinstance(vdep, tuple) else ("vv", vdep))
        r = OP("vector", "vv", lambda e, a=acc, mm=m: e.tensor_reduce(
            out=accr[:, mm:mm + 1], in_=a[:], axis=AXX, op=ALU.add))
        W(q_eng, ("vv", r))
        w1 = DMA(q_eng, fsem, scrh[m].ap(), accr[:, m:m + 1])
        W(q_eng, (fsem, w1))
        w2 = DMA(q_eng, fsem, rowb[:, 128 * m:128 * (m + 1)],
                 scrh[m].ap().rearrange("(a p) b -> a (p b)", a=1))
        W("vector", (fsem, w2))
        r = OP("vector", "vv", lambda e, mm=m: e.tensor_reduce(
            out=hrow[:, 4 * mm:4 * mm + 1], in_=rowb[:, 128 * mm:128 * (mm + 1)],
            axis=AXX, op=ALU.add))
        W("vector", ("vv", r))
        r = OP("vector", "vv", lambda e, mm=m: e.tensor_scalar(
            out=hrow[:, 4 * mm:4 * mm + 1], in0=hrow[:, 4 * mm:4 * mm + 1],
            scalar1=2.0 ** -25, scalar2=5e-9, op0=ALU.mult, op1=ALU.add))
        W("vector", ("vv", r))
        r = OP("vector", "vv", lambda e, mm=m: e.tensor_scalar(
            out=hrow[:, 4 * mm + 1:4 * mm + 2], in0=hrow[:, 4 * mm:4 * mm + 1],
            scalar1=-1.0, scalar2=None, op0=ALU.mult))
        W("vector", ("vv", r))
        r = OP("vector", "vv", lambda e, mm=m: e.tensor_scalar(
            out=hrow[:, 4 * mm + 2:4 * mm + 3], in0=hrow[:, 4 * mm:4 * mm + 1],
            scalar1=-2.0, scalar2=None, op0=ALU.mult))
        W(q_eng, ("vv", r))
        w3 = DMA(q_eng, fsem, scrh2[m].ap(), hrow[:, 4 * m:4 * m + 3])
        W(q_eng, (fsem, w3))
        w4 = DMA(q_eng, fsem, hc[:, 4 * m:4 * m + 3],
                 scrh2[m].ap().squeeze(0).partition_broadcast(128))
        return (fsem, w4)

    psm = ExitStack()
    ps_r = psm.enter_context(nc.psum_tensor("ps_r", [64, T], F32))
    ps_y = psm.enter_context(nc.psum_tensor("ps_y", [128, T], F32))
    ps_t = psm.enter_context(nc.psum_tensor("ps_t", [128, T], F32))
    W("tensor", ("ufm", ut_v), ("wbq", v_wb), ("vv", v_gfe))
    v_r = 0
    for hh in range(2):
        for c in range(DC):
            fn = lambda e, h=hh, cc=c: e.matmul(
                ps_r[0:64, 512 * h:512 * (h + 1)], wbq[:, 64 * cc:64 * (cc + 1)],
                u_fm[:, T * cc + 512 * h:T * cc + 512 * (h + 1)],
                start=(cc == 0), stop=(cc == 15))
            if hh == 1 and c == 15:
                v_r = OP("tensor", "tv", fn)
            else:
                OPq("tensor", fn)
    W("vector", ("tv", v_r))
    v_rbf = OP("vector", "vv", lambda e: e.tensor_copy(out=rbf[:], in_=ps_r[0:64, :]))

    vfree = [("vv", 0), ("vv", 0), ("vv", 0)]
    vq = [0, 0, 0]
    wfree = [("vv", 0), ("vv", 0), ("vv", 0)]
    wq3 = [0, 0, 0]
    wsl = [wstd[0], wstd[1], wstd2x]
    rlast = 0
    glast = 0
    dlast = 0
    for k in range(EC):
        for half in range(2):
            ec = 2 * k + half
            b = ec % 3
            W("sync", vfree[b])
            vq[b] = DMA("sync", f"ld{b}", wst[b][:],
                        wu_in[128 * (ec // 8):128 * (ec // 8 + 1),
                              1024 * (ec % 8):1024 * (ec % 8 + 1)])
            W("vector", (f"ld{b}", vq[b]))
            rlast = OP("vector", "vv", lambda e, bb=b, c=ec: e.tensor_reduce(
                out=acc_u[:, c:c + 1], in_=wst[bb][:], axis=AXX, op=ALU.add,
                apply_absolute_value=True))
            vfree[b] = ("vv", rlast)
        for kind in range(2):
            j = 2 * k + kind
            b = j % 3
            sl = wsl[b]
            sn = (f"lgg{b}" if kind == 0 else f"lgd{b}")
            if kind == 0:
                W("gpsimd", wfree[b])
                wq3[b] = DMA("gpsimd", sn, sl[:],
                             wg_in[128 * (k // 4):128 * (k // 4 + 1),
                                   2048 * (k % 4):2048 * (k % 4 + 1)])
                W("vector", (sn, wq3[b]))
                glast = OP("vector", "vv", lambda e, s_=sl, c=k: e.tensor_reduce(
                    out=acc_g[:, c:c + 1], in_=s_[:], axis=AXX, op=ALU.add,
                    apply_absolute_value=True))
                wfree[b] = ("vv", glast)
            else:
                W("scalar", wfree[b])
                wq3[b] = DMA("scalar", sn, sl[:], wd_in[128 * k:128 * (k + 1), :])
                W("vector", (sn, wq3[b]))
                dlast = OP("vector", "vv", lambda e, s_=sl, c=k: e.tensor_reduce(
                    out=acc_d[:, c:c + 1], in_=s_[:], axis=AXX, op=ALU.add,
                    apply_absolute_value=True))
                wfree[b] = ("vv", dlast)
    h_rdy_u = emit_fin(0, rlast, acc_u, "sync")
    h_rdy_g = emit_fin(1, ("vv", glast), acc_g, "sync")

    z_tok = P1(nc.sbuf_tensor("z_tok", [128, TT * D], BF16))
    z_tmp = P1(nc.sbuf_tensor("z_tmp", [128, 2 * T], F32))
    s_stage = P1(nc.sbuf_tensor("s_stage", [128, 2 * D], BF16))

    v_ztok = 0
    v_zh = [0, 0]
    ztf = [("tv", 0), ("tv", 0)]
    W("tensor", ("vv", v_rbf))
    for c in range(DC):
        zb = c % 2
        v_z = 0
        for hh in range(2):
            if v_zh[hh]:
                W("tensor", ("vv", v_zh[hh]))
            v_y = OP("tensor", "tv", lambda e, h=hh, cc=c: e.matmul(
                ps_y[:, 512 * h:512 * (h + 1)],
                gf[:, 128 * cc:128 * (cc + 1)], rbf[:, 512 * h:512 * (h + 1)],
                start=True, stop=True))
            W("vector", ("tv", v_y), ("vv", v_dp), *( [ztf[zb]] if hh == 0 else []))
            v_z = OP("vector", "vv", lambda e, cc=c, b=zb, h=hh: e.scalar_tensor_tensor(
                out=z_tmp[:, T * b + 512 * h:T * b + 512 * (h + 1)],
                in0=u_fm[:, T * cc + 512 * h:T * cc + 512 * (h + 1)],
                scalar=dp[:, cc:cc + 1], in1=ps_y[:, 512 * h:512 * (h + 1)],
                op0=ALU.mult, op1=ALU.add))
            v_zh[hh] = v_z
        W("tensor", ("vv", v_z), *( [("vv", v_ztok)] if v_ztok else []))
        for t_i in range(TT):
            v_tp = OP("tensor", "tv", lambda e, b=zb, t=t_i: e.transpose(
                ps_t[:, 128 * t:128 * (t + 1)],
                z_tmp[:, T * b + 128 * t:T * b + 128 * (t + 1)], ident[:]))
        ztf[zb] = ("tv", v_tp)
        W("vector", ("tv", v_tp))
        v_ztok = OP("vector", "vv", lambda e, cc=c: e.tensor_copy(
            out=z_tok[:].rearrange("p (t x) -> p t x", x=D)
            [:, 0:8, 128 * cc:128 * (cc + 1)],
            in_=ps_t[:].rearrange("p (t x) -> p t x", x=128)))

    W("vector", ("vv", v_ztok))
    v_mv = 0
    aggr_v = [0, 0]
    for t in range(TT):
        sl = t % 2
        if aggr_v[sl]:
            W("vector", ("vv", aggr_v[sl]))
        v_bn = 0
        for j in range(4):
            v_bn = OP("vector", "vv", lambda e, t_=t, j_=j, s_=sl: e.bn_stats(
                out=bnst[:, 24 * s_ + 6 * j_:24 * s_ + 6 * (j_ + 1)],
                in_=z_tok[:, D * t_ + 512 * j_:D * t_ + 512 * (j_ + 1)]))
        W("vector", ("vv", v_bn))
        v_mv = OP("vector", "vv", lambda e, t_=t, s_=sl: e.bn_aggr(
            out=mv[:, 2 * t_:2 * (t_ + 1)], in_=bnst[:, 24 * s_:24 * (s_ + 1)]))
        aggr_v[sl] = v_mv
    W("vector", ("vv", v_mv))
    v_ve = OP("vector", "vv", lambda e: e.tensor_scalar(
        out=sqv[:], in0=mv[:, 1:2 * TT:2], scalar1=EPS, scalar2=None, op0=ALU.add))
    W("scalar", ("vv", v_ve))
    v_sq = OP("scalar", "av", lambda e: e.activation(
        out=sqv[:], in_=sqv[:], func=AF.Sqrt))
    W("vector", ("av", v_sq))
    v_rs = OP("vector", "vv", lambda e: e.reciprocal(out=rstd[:], in_=sqv[:]))
    W("vector", ("vv", v_rs))
    sq = [0, 0]
    for t in range(TT):
        sb = t % 2
        W("vector", (f"ssc{sb}", sq[sb]))
        v_ap = OP("vector", "vv", lambda e, t_=t, b=sb: e.tensor_scalar(
            out=s_stage[:, D * b:D * (b + 1)], in0=z_tok[:, D * t_:D * (t_ + 1)],
            scalar1=mv[:, 2 * t_:2 * t_ + 1], scalar2=rstd[:, t_:t_ + 1],
            op0=ALU.subtract, op1=ALU.mult))
        W("scalar", ("vv", v_ap))
        sq[sb] = DMA("scalar", f"ssc{sb}", s_sc[128 * t:128 * (t + 1), :],
                     s_stage[:, D * sb:D * (sb + 1)])

    p1bar = {k: cnt.get(k, 0) for k in sems}

    v_sfm = 0
    W("scalar", ("ssc0", sq[0]), ("ssc1", sq[1]))
    for c in range(DC):
        v_sfm = DMAT("scalar", "sfm", s_fm[:, T * c:T * (c + 1)],
                     s_sc[0:T, 128 * c:128 * (c + 1)])

    psm.close()
    p1.close()

    ffn = ExitStack()
    FN = ffn.enter_context
    hid = [FN(nc.sbuf_tensor("hid0", [128, BEC * T], BF16))] * 2
    yac = FN(nc.sbuf_tensor("yac", [128, DC * T], BF16))
    sil_t = [FN(nc.sbuf_tensor(f"sil{i}", [128, T], BF16)) for i in range(2)]
    wdn = [FN(nc.sbuf_tensor(f"wdn{i}", [128, BEC * 256], BF16)) for i in range(2)]
    supsc = ExitStack()
    wupT = supsc.enter_context(nc.sbuf_tensor("wupT", [128, 2 * DC * SUPE], BF16))
    wgT = supsc.enter_context(nc.sbuf_tensor("wgT", [128, 2 * DC * SUPE], BF16))
    ps_a = [FN(nc.psum_tensor(f"ps_a{i}", [128, T], F32)) for i in range(4)]

    for eng in ("tensor", "vector", "scalar", "sync", "gpsimd"):
        W(eng, *[(k, v) for k, v in p1bar.items() if v])

    scr2_q = [(("qw0", 0), ("qw1", 0))] * NSLAB
    sup_ready = [((None, 0), (None, 0))] * NSUP
    mm_sup_done = [0] * NSUP
    u_free = [vfree[0], vfree[1], vfree[2], ("vv", 0), ("vv", 0), ("vv", 0)]
    d_free = [wfree[0], wfree[1]]
    dq = [0, 0]
    d_tq = [("qw0", 0), ("qw1", 0)]
    pair_last = {0: 0, 1: 0}
    state = {"uq": [0, 0, 0], "gq": [0, 0], "dq": [0, 0]}

    ldsem = ["tw0", "tw1", "tw2", "dt0", "dt1", "dt2"]
    ldval = [0] * 6

    def emit_ug_load(q, c, mat):
        idx = 2 * (DC * q + c) + mat
        b = idx % 6
        src_ap = (wu_in if mat == 0 else wg_in)[128 * c:128 * (c + 1),
                                                1024 * q:1024 * (q + 1)]
        W("gpsimd", u_free[b])
        ldval[b] = DMA("gpsimd", ldsem[b], wst[b][:], src_ap)

    def emit_ug_quant(q, c, mat):
        idx = 2 * (DC * q + c) + mat
        li = idx - 32 * q
        b = idx % 6
        dT = wupT if mat == 0 else wgT
        dst = dT[:].rearrange("p (s x) -> p s x", s=2)[:, :, SUPE * c:SUPE * (c + 1)]
        hp = h_u if mat == 0 else h_g
        hn = nh_u if mat == 0 else nh_g
        hrd = h_rdy_u if mat == 0 else h_rdy_g
        W("vector", (ldsem[b], ldval[b]), hrd,
          *( [("tv", mm_sup_done[2 * q - 1])] if q >= 1 and li < 6 else []))
        rg_ = OP("vector", "vv", lambda e, bb=b, d_=dst, hh_=hp: e.tensor_scalar(
            out=d_, in0=wst[bb][:].rearrange("p (s x) -> p s x", s=2),
            scalar1=hh_, scalar2=None, op0=ALU.is_gt))
        W("vector", ("vv", rg_))
        r = OP("vector", "vv", lambda e, bb=b, d_=dst, hh_=hn: e.scalar_tensor_tensor(
            out=d_, in0=wst[bb][:].rearrange("p (s x) -> p s x", s=2),
            scalar=hh_, in1=d_, op0=ALU.is_lt, op1=ALU.subtract))
        u_free[b] = ("vv", r)
        pair_last[mat] = r
        if li == 31:
            sup_ready[2 * q] = sup_ready[2 * q + 1] = (
                ("vv", pair_last[0]), ("vv", pair_last[1]))

    def emit_one_down_tile(i, h_rdy_d):
        b = i % 2
        W("gpsimd", d_free[b])
        state["dq"][b] = DMA("gpsimd", f"qd{b}", wstd[b][:],
                             wd_in[128 * i:128 * (i + 1), :])
        W("vector", (f"qd{b}", state["dq"][b]), d_tq[b], h_rdy_d)
        rg_ = OP("vector", "vv", lambda e, bb=b: e.tensor_scalar(
            out=terd_t[bb][:], in0=wstd[bb][:], scalar1=h_d, scalar2=None,
            op0=ALU.is_gt))
        W("vector", ("vv", rg_))
        r = OP("vector", "vv", lambda e, bb=b: e.scalar_tensor_tensor(
            out=terd_t[bb][:], in0=wstd[bb][:], scalar=nh_d, in1=terd_t[bb][:],
            op0=ALU.is_lt, op1=ALU.subtract))
        d_free[b] = ("vv", r)
        W("gpsimd", ("vv", r))
        w = DMA("gpsimd", f"qw{b}", scr2_d[128 * i:128 * (i + 1), :], terd_t[b][:])
        d_tq[b] = (f"qw{b}", w)

    v_hid = [0] * (EC + 2)
    state_v = {"silu": 0, "flush": 0, "z2tok": 0}
    flush_v = [0] * DC
    dn_free = [("tv", 0)] * 2
    h_rdy_d_v = [("vv", 0)]

    def emit_A_block(b, first_waits, steps):
        si_extra = [0, 0]
        for si in range(2):
            s = 2 * b + si
            W("tensor", sup_ready[s][0], sup_ready[s][1],
              *(first_waits if si == 0 else []))
            for ci in range(SUPC):
                ec = SUPC * s + ci
                ecb = ec % BEC
                pa = ps_a[2 * (ec % 2)]
                pg = ps_a[2 * (ec % 2) + 1]
                if ec >= 2:
                    W("tensor", ("vv", v_hid[ec - 2]))
                v_mmu = v_mmg = 0
                for hh in range(2):
                    for c in range(DC):
                        fn = lambda e, p_=pa, cc=c, h=hh, s_=s, ci_=ci: e.matmul(
                            p_[:, 512 * h:512 * (h + 1)],
                            wupT[:, 8192 * (s_ % 2) + SUPE * cc + 128 * ci_:
                                 8192 * (s_ % 2) + SUPE * cc + 128 * (ci_ + 1)],
                            s_fm[:, T * cc + 512 * h:T * cc + 512 * (h + 1)],
                            start=(cc == 0), stop=(cc == 15))
                        if hh == 1 and c == 15:
                            v_mmu = OP("tensor", "tv", fn)
                        else:
                            OPq("tensor", fn)
                for hh in range(2):
                    for c in range(DC):
                        fn = lambda e, p_=pg, cc=c, h=hh, s_=s, ci_=ci: e.matmul(
                            p_[:, 512 * h:512 * (h + 1)],
                            wgT[:, 8192 * (s_ % 2) + SUPE * cc + 128 * ci_:
                                8192 * (s_ % 2) + SUPE * cc + 128 * (ci_ + 1)],
                            s_fm[:, T * cc + 512 * h:T * cc + 512 * (h + 1)],
                            start=(cc == 0), stop=(cc == 15))
                        if hh == 1 and c == 15:
                            v_mmg = OP("tensor", "tv", fn)
                        else:
                            OPq("tensor", fn)
                mm_sup_done[s] = v_mmg
                sb2 = ec % 2
                W("scalar", ("tv", v_mmu), h_rdy_u,
                  *( [("vv", v_hid[ec - 2])] if ec >= 2 else []))
                state_v["silu"] = OP("scalar", "av", lambda e, p_=pa, i=sb2: e.activation(
                    out=sil_t[i][:], in_=p_[:], func=AF.Silu, scale=nsu))
                W("vector", ("av", state_v["silu"]), ("tv", v_mmg), h_rdy_g)
                v_hid[ec] = OP("vector", "vv",
                               lambda e, p_=pg, i=sb2, eb=ecb, hb=b % 2:
                               e.scalar_tensor_tensor(
                                   out=hid[hb][:, T * eb:T * (eb + 1)], in0=p_[:],
                                   scalar=nsg, in1=sil_t[i][:],
                                   op0=ALU.mult, op1=ALU.mult))
                for _ in range(6):
                    if steps:
                        steps.pop(0)()
        while steps:
            steps.pop(0)()

    def emit_B_block(b, z2_parts, ug_steps):
        dnq = [0, 0]
        z2prev = ("vv", 0)
        for c in range(DC):
            g4 = c // 2
            db = g4 % 2
            pd = ps_a[2 * (c % 2)]
            if c % 2 == 0:
                W("sync", scr2_q[b][0], scr2_q[b][1], dn_free[db])
                dnq[db] = DMA("sync", f"wd{db}",
                              wdn[db][:].rearrange("p (k j) -> p k j", j=256),
                              scr2_d[1024 * b:1024 * (b + 1), 256 * g4:256 * (g4 + 1)]
                              .rearrange("(k p) j -> p k j", p=128))
            W("tensor", (f"wd{db}", dnq[db]), ("vv", v_hid[BEC * b + BEC - 1]))
            if c >= 2:
                W("tensor", ("vv", state_v["flush"]))
            else:
                W("tensor", ("av", state_v["silu"]))
            v_mmd = 0
            for hh in range(2):
                for k in range(BEC):
                    fn = lambda e, p_=pd, kk=k, h=hh, d_=db, j_=c % 2, hb=b % 2: e.matmul(
                        p_[:, 512 * h:512 * (h + 1)],
                        wdn[d_][:, 256 * kk + 128 * j_:256 * kk + 128 * (j_ + 1)],
                        hid[hb][:, T * kk + 512 * h:T * kk + 512 * (h + 1)],
                        start=(kk == 0), stop=(kk == BEC - 1))
                    if hh == 1 and k == BEC - 1:
                        v_mmd = OP("tensor", "tv", fn)
                    else:
                        OPq("tensor", fn)
            if c % 2 == 1:
                dn_free[db] = ("tv", v_mmd)
            if b == 0:
                W("vector", ("tv", v_mmd), h_rdy_d_v[0])
                flush_v[c] = OP("vector", "vv",
                                lambda e, p_=pd, cc=c: e.scalar_tensor_tensor(
                                    out=yac[:, T * cc:T * (cc + 1)], in0=p_[:],
                                    scalar=nsd, in1=s_fm[:, T * cc:T * (cc + 1)],
                                    op0=ALU.mult, op1=ALU.add))
                state_v["flush"] = flush_v[c]
                for _ in range(3):
                    if ug_steps:
                        ug_steps.pop(0)()
            elif b < NB - 1:
                W("vector", ("tv", v_mmd), ("vv", flush_v[c]))
                flush_v[c] = OP("vector", "vv",
                                lambda e, p_=pd, cc=c: e.scalar_tensor_tensor(
                                    out=yac[:, T * cc:T * (cc + 1)], in0=p_[:],
                                    scalar=nsd, in1=yac[:, T * cc:T * (cc + 1)],
                                    op0=ALU.mult, op1=ALU.add))
                state_v["flush"] = flush_v[c]
                for _ in range(3):
                    if ug_steps:
                        ug_steps.pop(0)()
            else:
                z2f, z2tok = z2_parts
                W("vector", ("tv", v_mmd), ("vv", flush_v[c]), z2prev)
                v_z2 = OP("vector", "vv", lambda e, p_=pd, cc=c: e.scalar_tensor_tensor(
                    out=z2f[:], in0=p_[:], scalar=nsd,
                    in1=yac[:, T * cc:T * (cc + 1)], op0=ALU.mult, op1=ALU.add))
                W("tensor", ("vv", v_z2))
                for half in range(2):
                    pt_ = ps_a[1 + 2 * half]
                    for j in range(4):
                        t_i = 4 * half + j
                        v_tp2 = OP("tensor", "tv", lambda e, p_=pt_, t=t_i: e.transpose(
                            p_[:, 128 * (t % 4):128 * (t % 4 + 1)],
                            z2f[:, 128 * t:128 * (t + 1)], ident[:]))
                    W("vector", ("tv", v_tp2))
                    vzt = OP("vector", "vv", lambda e, p_=pt_, cc=c, hf=half: e.tensor_copy(
                        out=z2tok[:].rearrange("p (t x) -> p t x", x=D)
                        [:, 4 * hf:4 * (hf + 1), 128 * cc:128 * (cc + 1)],
                        in_=p_[:, 0:512].rearrange("p (t x) -> p t x", x=128)))
                z2prev = ("vv", vzt)
                state_v["flush"] = vzt
                state_v["z2tok"] = vzt
                for _ in range(3):
                    if ug_steps:
                        ug_steps.pop(0)()
        while ug_steps:
            ug_steps.pop(0)()

    ptiles = [(0, cc, mat) for cc in range(DC) for mat in range(2)]
    for t6 in ptiles[:6]:
        emit_ug_load(*t6)
    for ki, t in enumerate(ptiles):
        emit_ug_quant(*t)
        if ki + 6 < 32:
            emit_ug_load(*ptiles[ki + 6])

    first_A = [("sfm", v_sfm), ("misc", v_ident)]
    z2_parts = None
    for b in range(NB):
        tiles = ([(b + 1, cc, mat) for cc in range(DC) for mat in range(2)]
                 if b + 1 < NB else [])
        steps = []
        for t6 in tiles[:6]:
            steps.append(lambda t=t6: emit_ug_load(*t))
        if b == 0:
            def fin2_step():
                h_rdy_d_v[0] = emit_fin(2, ("vv", dlast), acc_d, "gpsimd")
            steps.append(fin2_step)
            for m_ in range(3):
                for k_ in range(BEC):
                    steps.append(lambda i_=BEC * m_ + k_:
                                 emit_one_down_tile(i_, h_rdy_d_v[0]))
                steps.append(lambda m=m_: scr2_q.__setitem__(
                    m, (d_tq[0], d_tq[1])))
        if b >= 1 and b + 2 < NSLAB:
            for k in range(BEC):
                steps.append(lambda i_=BEC * (b + 2) + k:
                             emit_one_down_tile(i_, h_rdy_d_v[0]))
        ug_steps = []

        def mk_step(ki, t, tl):
            def f():
                emit_ug_quant(*t)
                if ki + 6 < len(tl):
                    emit_ug_load(*tl[ki + 6])
            return f
        for ki, t in enumerate(tiles):
            ug_steps.append(mk_step(ki, t, tiles))
        emit_A_block(b, first_A if b == 0 else
                     [("vv", state_v["flush"]), ("av", state_v["silu"])], steps)
        if b + 2 < NSLAB and b >= 1:
            scr2_q[b + 2] = (d_tq[0], d_tq[1])
        if b == NB - 1:
            supsc.close()
            z2f = FN(nc.sbuf_tensor("z2f", [128, T], F32))
            z2tok = FN(nc.sbuf_tensor("z2tok", [128, TT * D], BF16))
            ostage = FN(nc.sbuf_tensor("ost", [128, D], F32))
            z2_parts = (z2f, z2tok)
            emit_B_block(b, z2_parts, ug_steps)
        else:
            emit_B_block(b, None, ug_steps)

    W("vector", ("vv", state_v["z2tok"]))
    v_mv2 = 0
    aggr_v = [0, 0]
    for t in range(TT):
        sl = t % 2
        if aggr_v[sl]:
            W("vector", ("vv", aggr_v[sl]))
        v_bn2 = 0
        for j in range(4):
            v_bn2 = OP("vector", "vv", lambda e, t_=t, j_=j, s_=sl: e.bn_stats(
                out=bnst[:, 24 * s_ + 6 * j_:24 * s_ + 6 * (j_ + 1)],
                in_=z2tok[:, D * t_ + 512 * j_:D * t_ + 512 * (j_ + 1)]))
        W("vector", ("vv", v_bn2))
        v_mv2 = OP("vector", "vv", lambda e, t_=t, s_=sl: e.bn_aggr(
            out=mv[:, 2 * t_:2 * (t_ + 1)], in_=bnst[:, 24 * s_:24 * (s_ + 1)]))
        aggr_v[sl] = v_mv2
    W("vector", ("vv", v_mv2))
    v_ve2 = OP("vector", "vv", lambda e: e.tensor_scalar(
        out=sqv[:], in0=mv[:, 1:2 * TT:2], scalar1=EPS, scalar2=None, op0=ALU.add))
    W("scalar", ("vv", v_ve2))
    v_sq2 = OP("scalar", "av", lambda e: e.activation(
        out=sqv[:], in_=sqv[:], func=AF.Sqrt))
    W("vector", ("av", v_sq2))
    v_rs2 = OP("vector", "vv", lambda e: e.reciprocal(out=rstd[:], in_=sqv[:]))
    W("vector", ("vv", v_rs2))
    oq = 0
    for t in range(TT):
        W("vector", ("out0", oq))
        v_o = OP("vector", "vv", lambda e, t_=t: e.tensor_scalar(
            out=ostage[:], in0=z2tok[:, D * t_:D * (t_ + 1)],
            scalar1=mv[:, 2 * t_:2 * t_ + 1], scalar2=rstd[:, t_:t_ + 1],
            op0=ALU.subtract, op1=ALU.mult))
        W("sync", ("vv", v_o))
        oq = DMA("sync", "out0", out[128 * t:128 * (t + 1), :], ostage[:])
    W("sync", ("out0", oq))
    W("gpsimd", ("out0", oq))

    with nc.Block() as block:
        @block.sync
        def _(e):
            for f in ops["sync"]:
                f(e)

        @block.scalar
        def _(e):
            for f in ops["scalar"]:
                f(e)

        @block.vector
        def _(e):
            for f in ops["vector"]:
                f(e)

        @block.tensor
        def _(e):
            for f in ops["tensor"]:
                f(e)

        @block.gpsimd
        def _(e):
            for f in ops["gpsimd"]:
                f(e)

    ffn.close()
    es.close()
    return nc


_PT = np.zeros((8, 2, 256), np.float32)
for _lvl in range(8):
    _bits = (np.arange(256) >> (7 - _lvl)) & 1
    _PT[_lvl, 0, _bits == 0] = 1.0
    _PT[_lvl, 1, _bits == 1] = 1.0


def kernel(L0_attn_u, L0_attn_A_factors, L0_attn_W_b_rank, L0_attn_W_b_state,
           L0_attn_W_c_rank, L0_attn_W_c_model, L0_attn_D_vec,
           L0_ffn_W_up, L0_ffn_W_gate, L0_ffn_W_down):
    u = np.ascontiguousarray(np.asarray(L0_attn_u, np.float32).reshape(NCORES * T, D))
    at = np.ascontiguousarray(np.asarray(L0_attn_A_factors, np.float32).transpose(0, 2, 1))
    wbs_t = np.ascontiguousarray(np.asarray(L0_attn_W_b_state, np.float32).T)
    dvec = np.ascontiguousarray(np.asarray(L0_attn_D_vec, np.float32).reshape(DC, 128).T)

    nc = build_core()
    common = {
        "w_b_rank": np.ascontiguousarray(np.asarray(L0_attn_W_b_rank, np.float32)),
        "a_t": at, "pt_all": _PT, "w_bs_t": wbs_t,
        "w_c_rank": np.ascontiguousarray(np.asarray(L0_attn_W_c_rank, np.float32)),
        "w_c_model": np.ascontiguousarray(np.asarray(L0_attn_W_c_model, np.float32)),
        "d_vec": dvec,
        "w_up": np.ascontiguousarray(np.asarray(L0_ffn_W_up, np.float32).T),
        "w_gate": np.ascontiguousarray(np.asarray(L0_ffn_W_gate, np.float32).T),
        "w_down": np.ascontiguousarray(np.asarray(L0_ffn_W_down, np.float32).T),
        "ident": np.eye(128, dtype=np.float32),
    }
    in_maps = [dict(common, u=np.ascontiguousarray(u[T * i:T * (i + 1)]))
               for i in range(NCORES)]
    res = run_bass_kernel_spmd(nc, in_maps, core_ids=list(range(NCORES)))
    full = np.concatenate([np.asarray(r["out"]) for r in res.results], 0)
    return full.reshape(2, 4096, 2048).astype(np.float32)


# version 28
# speedup vs baseline: 1.0792x; 1.0792x over previous
import numpy as np
import concourse.bass as bass
import concourse.mybir as mybir
from concourse.bass_utils import run_bass_kernel_spmd
from concourse.alu_op_type import AluOpType as ALU
from contextlib import ExitStack

F32, BF16 = mybir.dt.float32, mybir.dt.bfloat16
AF = mybir.ActivationFunctionType
AXX = mybir.AxisListType.X

NCORES = 8
T = 1024
TT = 8
D = 2048
DC = 16
E = 8192
EC = 64
NB = 8
BEC = 8
NSUP = 16
SUPE = 512
SUPC = 4
NSLAB = 8
EPS = 1e-5


def build_core():
    nc = bass.Bass("TRN2", target_bir_lowering=False, debug=False)

    u_in = nc.dram_tensor("u", [T, D], F32, kind="ExternalInput").ap()
    wb_in = nc.dram_tensor("w_b_rank", [D, 64], F32, kind="ExternalInput").ap()
    at_in = nc.dram_tensor("a_t", [8, 2, 2], F32, kind="ExternalInput").ap()
    pt_in = nc.dram_tensor("pt_all", [8, 2, 256], F32, kind="ExternalInput").ap()
    wbsT_in = nc.dram_tensor("w_bs_t", [256, 64], F32, kind="ExternalInput").ap()
    wcr_in = nc.dram_tensor("w_c_rank", [256, 64], F32, kind="ExternalInput").ap()
    wcm_in = nc.dram_tensor("w_c_model", [64, D], F32, kind="ExternalInput").ap()
    dvec_in = nc.dram_tensor("d_vec", [128, DC], F32, kind="ExternalInput").ap()
    wu_in = nc.dram_tensor("w_up", [D, E], F32, kind="ExternalInput").ap()
    wg_in = nc.dram_tensor("w_gate", [D, E], F32, kind="ExternalInput").ap()
    wd_in = nc.dram_tensor("w_down", [E, D], F32, kind="ExternalInput").ap()
    ident_in = nc.dram_tensor("ident", [128, 128], F32, kind="ExternalInput").ap()
    out = nc.dram_tensor("out", [T, D], F32, kind="ExternalOutput").ap()

    u_bf = nc.dram_tensor("u_bf_scr", [T, D], BF16)
    s_sc = nc.dram_tensor("s_scr", [T, D], BF16)
    scr2_d = nc.dram_tensor("scr2_d_scr", [E, D], BF16)
    scrh = [nc.dram_tensor(f"scrh{m}", [128, 1], F32) for m in range(3)]
    scrh2 = [nc.dram_tensor(f"scrh2_{m}", [1, 3], F32) for m in range(3)]

    ops = {e: [] for e in ("sync", "scalar", "vector", "tensor", "gpsimd")}
    cnt = {}
    es = ExitStack()
    EN = es.enter_context
    semnames = (["tv", "vv", "av", "misc", "ubf", "ufm", "wbq", "ssc0", "ssc1",
                 "sfm", "out0", "fin0", "fin1", "fin2", "msc2", "msc3",
                 "dtw0", "dtw1"]
                + [f"{p}{i}" for p in ("ldd", "lgg", "lgd", "tdw", "dt",
                                       "trU", "trG", "qd", "qw", "s2g") for i in range(3)]
                + [f"{p}{i}" for p in ("ld", "tw") for i in range(3)]
                + [f"wd{i}" for i in range(4)])
    sems = {n: EN(nc.semaphore(n)) for n in semnames}

    def W(eng, *deps):
        for sname, v in deps:
            if v and v > 0:
                ops[eng].append(lambda e, s=sems[sname], v_=v: e.wait_ge(s, v_))

    def DMA(eng, qname, outap, inap, **kw):
        cnt[qname] = cnt.get(qname, 0) + 16
        ops[eng].append(lambda e, o=outap, i=inap, s=sems[qname], kw_=kw:
                        e.dma_start(out=o, in_=i, **kw_).then_inc(s, 16))
        return cnt[qname]

    def DMAT(eng, qname, outap, inap):
        cnt[qname] = cnt.get(qname, 0) + 16
        ops[eng].append(lambda e, o=outap, i=inap, s=sems[qname]:
                        e.dma_start_transpose(out=o, in_=i).then_inc(s, 16))
        return cnt[qname]

    def OP(eng, sname, fn):
        cnt[sname] = cnt.get(sname, 0) + 1
        ops[eng].append(lambda e, f=fn, s=sems[sname]: f(e).then_inc(s, 1))
        return cnt[sname]

    def OPq(eng, fn):
        ops[eng].append(lambda e, f=fn: f(e))

    s_fm = EN(nc.sbuf_tensor("s_fm", [128, DC * T], BF16))
    ident = EN(nc.sbuf_tensor("ident_s", [128, 128], F32))
    hc = EN(nc.sbuf_tensor("hc", [128, 12], F32))
    hrow = EN(nc.sbuf_tensor("hrow", [1, 16], F32))
    rowb = EN(nc.sbuf_tensor("rowb", [1, 3 * 128], F32))
    acc_u = EN(nc.sbuf_tensor("acc_u", [128, 2 * EC], F32))
    acc_g = EN(nc.sbuf_tensor("acc_g", [128, EC], F32))
    acc_d = EN(nc.sbuf_tensor("acc_d", [128, EC], F32))
    accr = EN(nc.sbuf_tensor("accr", [128, 4], F32))
    bnst = EN(nc.sbuf_tensor("bnst", [128, 2 * 24], F32))
    mv = EN(nc.sbuf_tensor("mv", [128, TT * 2], F32))
    rstd = EN(nc.sbuf_tensor("rstd", [128, TT], F32))
    sqv = EN(nc.sbuf_tensor("sqv", [128, TT], F32))
    wst = [EN(nc.sbuf_tensor(f"wst{i}", [128, 1024], F32)) for i in range(6)]
    wstd = [EN(nc.sbuf_tensor(f"wstd{i}", [128, 2048], F32)) for i in range(2)]
    terd_t = [EN(nc.sbuf_tensor(f"terd_t{i}", [128, 2048], BF16)) for i in range(2)]

    h_u, nh_u, nsu = hc[:, 0:1], hc[:, 1:2], hc[:, 2:3]
    h_g, nh_g, nsg = hc[:, 4:5], hc[:, 5:6], hc[:, 6:7]
    h_d, nh_d, nsd = hc[:, 8:9], hc[:, 9:10], hc[:, 10:11]

    v_ident = DMA("sync", "misc", ident[:], ident_in)

    p1 = ExitStack()
    P1 = p1.enter_context
    u_fm = P1(nc.sbuf_tensor("u_fm", [128, DC * T], BF16))
    wbq = P1(nc.sbuf_tensor("wbq", [128, DC * 64], BF16))
    gf = P1(nc.sbuf_tensor("gf", [64, D], BF16))
    dp = P1(nc.sbuf_tensor("dp", [128, DC], F32))
    dvc = P1(nc.sbuf_tensor("dvc", [128, DC], F32))
    rbf = P1(nc.sbuf_tensor("rbf", [64, T], BF16))
    wstd2x = P1(nc.sbuf_tensor("wstd2x", [128, 2048], F32))

    v_dvc = DMA("sync", "msc2", dvc[:], dvec_in)
    v_wb = DMA("gpsimd", "wbq", wbq[:].rearrange("p (c r) -> p c r", r=64),
               wb_in.rearrange("(c p) r -> p c r", p=128))
    W("vector", ("msc2", v_dvc))
    v_dp = OP("vector", "vv", lambda e: e.tensor_scalar(
        out=dp[:], in0=dvc[:], scalar1=1.0, scalar2=None, op0=ALU.add))
    ucst = P1(nc.sbuf_tensor("ucst", [128, 2048], F32))
    ucb = P1(nc.sbuf_tensor("ucb", [128, 2048], BF16))
    vcast = 0
    vwr = 0
    for j in range(TT):
        if vcast:
            W("scalar", ("vv", vcast))
        vu = DMA("scalar", "ubf", ucst[:], u_in[128 * j:128 * (j + 1), :])
        W("vector", ("ubf", vu), *( [("ufm", vwr)] if vwr else []))
        vcast = OP("vector", "vv", lambda e: e.tensor_copy(out=ucb[:], in_=ucst[:]))
        W("scalar", ("vv", vcast))
        vwr = DMA("scalar", "ufm", u_bf[128 * j:128 * (j + 1), :], ucb[:])
    ut_v = 0
    W("scalar", ("ufm", vwr))
    for c in range(DC):
        ut_v = DMAT("scalar", "ufm", u_fm[:, T * c:T * (c + 1)],
                    u_bf[0:T, 128 * c:128 * (c + 1)])

    fold = ExitStack()
    FE = fold.enter_context
    at_s = FE(nc.sbuf_tensor("at_s", [2, 16], F32))
    pt_s = FE(nc.sbuf_tensor("pt_s", [2, 8 * 256], F32))
    t1_s = FE(nc.sbuf_tensor("t1_s", [2, 256], F32))
    m_s = FE(nc.sbuf_tensor("m_s", [128, 512], F32))
    wbsT_s = FE(nc.sbuf_tensor("wbsT_s", [128, 2 * 64], F32))
    wcr_s = FE(nc.sbuf_tensor("wcr_s", [128, 2 * 64], F32))
    wcm_s = FE(nc.sbuf_tensor("wcm_s", [64, D], F32))
    g1_s = FE(nc.sbuf_tensor("g1_s", [64, 256], F32))
    g1t_s = FE(nc.sbuf_tensor("g1t_s", [128, 2 * 64], F32))
    g2t_s = FE(nc.sbuf_tensor("g2t_s", [64, 64], F32))
    psf = ExitStack()
    ps_f = psf.enter_context(nc.psum_tensor("ps_f", [128, 512], F32))
    ps_f2 = psf.enter_context(nc.psum_tensor("ps_f2", [64, 512], F32))

    DMA("sync", "msc3", at_s[:].rearrange("j (l i) -> j l i", i=2),
        at_in.rearrange("l j i -> j l i"))
    DMA("sync", "msc3", pt_s[:].rearrange("j (l n) -> j l n", n=256),
        pt_in.rearrange("l j n -> j l n"))
    DMA("sync", "msc3", wbsT_s[:].rearrange("p (k m) -> p k m", m=64),
        wbsT_in.rearrange("(k p) m -> p k m", p=128))
    DMA("sync", "msc3", wcr_s[:].rearrange("p (k m) -> p k m", m=64),
        wcr_in.rearrange("(k p) m -> p k m", p=128))
    ld_fold = DMA("sync", "msc3", wcm_s[:], wcm_in)

    W("tensor", ("msc3", ld_fold), ("misc", v_ident))
    v_m = 0
    for lvl in range(8):
        if v_m:
            W("tensor", ("vv", v_m))
        v_t1 = OP("tensor", "tv", lambda e, l=lvl: e.matmul(
            ps_f2[0:2, 0:256], at_s[:, 2 * l:2 * (l + 1)],
            pt_s[:, 256 * l:256 * (l + 1)], start=True, stop=True))
        W("vector", ("tv", v_t1), ("vv", v_m))
        v_t1e = OP("vector", "vv", lambda e: e.tensor_copy(
            out=t1_s[:], in_=ps_f2[0:2, 0:256]))
        W("tensor", ("vv", v_t1e))
        for ch in range(2):
            v_c = OP("tensor", "tv", lambda e, l=lvl, c=ch: e.matmul(
                ps_f[:, 256 * c:256 * (c + 1)],
                pt_s[:, 256 * l + 128 * c:256 * l + 128 * (c + 1)],
                t1_s[:], start=True, stop=True))
        W("vector", ("tv", v_c))
        if lvl == 0:
            v_m = OP("vector", "vv", lambda e: e.tensor_copy(out=m_s[:], in_=ps_f[:]))
        else:
            v_m = OP("vector", "vv", lambda e: e.tensor_tensor(
                out=m_s[:], in0=m_s[:], in1=ps_f[:], op=ALU.mult))
    W("tensor", ("vv", v_m))
    OPq("tensor", lambda e: e.matmul(
        ps_f2[0:64, 0:256], wbsT_s[:, 0:64], m_s[:, 0:256], start=True, stop=False))
    v_g1 = OP("tensor", "tv", lambda e: e.matmul(
        ps_f2[0:64, 0:256], wbsT_s[:, 64:128], m_s[:, 256:512], start=False, stop=True))
    W("vector", ("tv", v_g1))
    v_g1e = OP("vector", "vv", lambda e: e.tensor_copy(
        out=g1_s[:], in_=ps_f2[0:64, 0:256]))
    W("tensor", ("vv", v_g1e))
    for k in range(2):
        v_tp = OP("tensor", "tv", lambda e, kk=k: e.transpose(
            ps_f[:, 64 * kk:64 * (kk + 1)], g1_s[:, 128 * kk:128 * (kk + 1)],
            ident[0:64, 0:64]))
    W("vector", ("tv", v_tp))
    v_g1t = OP("vector", "vv", lambda e: e.tensor_copy(
        out=g1t_s[:], in_=ps_f[:, 0:128]))
    W("tensor", ("vv", v_g1t))
    OPq("tensor", lambda e: e.matmul(
        ps_f2[0:64, 0:64], wcr_s[:, 0:64], g1t_s[:, 0:64], start=True, stop=False))
    v_g2 = OP("tensor", "tv", lambda e: e.matmul(
        ps_f2[0:64, 0:64], wcr_s[:, 64:128], g1t_s[:, 64:128], start=False, stop=True))
    W("vector", ("tv", v_g2))
    v_g2e = OP("vector", "vv", lambda e: e.tensor_copy(
        out=g2t_s[:], in_=ps_f2[0:64, 0:64]))
    W("tensor", ("vv", v_g2e))
    v_gfe = 0
    for k in range(4):
        if v_gfe:
            W("tensor", ("vv", v_gfe))
        v_gf = OP("tensor", "tv", lambda e, kk=k: e.matmul(
            ps_f[0:64, 0:512], g2t_s[:], wcm_s[:, 512 * kk:512 * (kk + 1)],
            start=True, stop=True))
        W("vector", ("tv", v_gf))
        v_gfe = OP("vector", "vv", lambda e, kk=k: e.tensor_copy(
            out=gf[:, 512 * kk:512 * (kk + 1)], in_=ps_f[0:64, 0:512]))
    psf.close()
    fold.close()

    def emit_fin(m, vdep, acc, q_eng):
        fsem = f"fin{m}"
        W("vector", vdep if isinstance(vdep, tuple) else ("vv", vdep))
        r = OP("vector", "vv", lambda e, a=acc, mm=m: e.tensor_reduce(
            out=accr[:, mm:mm + 1], in_=a[:], axis=AXX, op=ALU.add))
        W(q_eng, ("vv", r))
        w1 = DMA(q_eng, fsem, scrh[m].ap(), accr[:, m:m + 1])
        W(q_eng, (fsem, w1))
        w2 = DMA(q_eng, fsem, rowb[:, 128 * m:128 * (m + 1)],
                 scrh[m].ap().rearrange("(a p) b -> a (p b)", a=1))
        W("vector", (fsem, w2))
        r = OP("vector", "vv", lambda e, mm=m: e.tensor_reduce(
            out=hrow[:, 4 * mm:4 * mm + 1], in_=rowb[:, 128 * mm:128 * (mm + 1)],
            axis=AXX, op=ALU.add))
        W("vector", ("vv", r))
        r = OP("vector", "vv", lambda e, mm=m: e.tensor_scalar(
            out=hrow[:, 4 * mm:4 * mm + 1], in0=hrow[:, 4 * mm:4 * mm + 1],
            scalar1=2.0 ** -25, scalar2=5e-9, op0=ALU.mult, op1=ALU.add))
        W("vector", ("vv", r))
        r = OP("vector", "vv", lambda e, mm=m: e.tensor_scalar(
            out=hrow[:, 4 * mm + 1:4 * mm + 2], in0=hrow[:, 4 * mm:4 * mm + 1],
            scalar1=-1.0, scalar2=None, op0=ALU.mult))
        W("vector", ("vv", r))
        r = OP("vector", "vv", lambda e, mm=m: e.tensor_scalar(
            out=hrow[:, 4 * mm + 2:4 * mm + 3], in0=hrow[:, 4 * mm:4 * mm + 1],
            scalar1=-2.0, scalar2=None, op0=ALU.mult))
        W(q_eng, ("vv", r))
        w3 = DMA(q_eng, fsem, scrh2[m].ap(), hrow[:, 4 * m:4 * m + 3])
        W(q_eng, (fsem, w3))
        w4 = DMA(q_eng, fsem, hc[:, 4 * m:4 * m + 3],
                 scrh2[m].ap().squeeze(0).partition_broadcast(128))
        return (fsem, w4)

    psm = ExitStack()
    ps_r = psm.enter_context(nc.psum_tensor("ps_r", [64, T], F32))
    ps_y = psm.enter_context(nc.psum_tensor("ps_y", [128, T], F32))
    ps_t = psm.enter_context(nc.psum_tensor("ps_t", [128, T], F32))
    W("tensor", ("ufm", ut_v), ("wbq", v_wb), ("vv", v_gfe))
    v_r = 0
    for hh in range(2):
        for c in range(DC):
            fn = lambda e, h=hh, cc=c: e.matmul(
                ps_r[0:64, 512 * h:512 * (h + 1)], wbq[:, 64 * cc:64 * (cc + 1)],
                u_fm[:, T * cc + 512 * h:T * cc + 512 * (h + 1)],
                start=(cc == 0), stop=(cc == 15))
            if hh == 1 and c == 15:
                v_r = OP("tensor", "tv", fn)
            else:
                OPq("tensor", fn)
    W("vector", ("tv", v_r))
    v_rbf = OP("vector", "vv", lambda e: e.tensor_copy(out=rbf[:], in_=ps_r[0:64, :]))

    vfree = [("vv", 0), ("vv", 0), ("vv", 0)]
    vq = [0, 0, 0]
    wfree = [("vv", 0), ("vv", 0), ("vv", 0)]
    wq3 = [0, 0, 0]
    wsl = [wstd[0], wstd[1], wstd2x]
    rlast = 0
    glast = 0
    dlast = 0
    for k in range(EC):
        for half in range(2):
            ec = 2 * k + half
            b = ec % 3
            W("sync", vfree[b])
            vq[b] = DMA("sync", f"ld{b}", wst[b][:],
                        wu_in[128 * (ec // 8):128 * (ec // 8 + 1),
                              1024 * (ec % 8):1024 * (ec % 8 + 1)])
            W("vector", (f"ld{b}", vq[b]))
            rlast = OP("vector", "vv", lambda e, bb=b, c=ec: e.tensor_reduce(
                out=acc_u[:, c:c + 1], in_=wst[bb][:], axis=AXX, op=ALU.add,
                apply_absolute_value=True))
            vfree[b] = ("vv", rlast)
        for kind in range(2):
            j = 2 * k + kind
            b = j % 3
            sl = wsl[b]
            sn = (f"lgg{b}" if kind == 0 else f"lgd{b}")
            if kind == 0:
                W("gpsimd", wfree[b])
                wq3[b] = DMA("gpsimd", sn, sl[:],
                             wg_in[128 * (k // 4):128 * (k // 4 + 1),
                                   2048 * (k % 4):2048 * (k % 4 + 1)])
                W("vector", (sn, wq3[b]))
                glast = OP("vector", "vv", lambda e, s_=sl, c=k: e.tensor_reduce(
                    out=acc_g[:, c:c + 1], in_=s_[:], axis=AXX, op=ALU.add,
                    apply_absolute_value=True))
                wfree[b] = ("vv", glast)
            else:
                W("scalar", wfree[b])
                wq3[b] = DMA("scalar", sn, sl[:], wd_in[128 * k:128 * (k + 1), :])
                W("vector", (sn, wq3[b]))
                dlast = OP("vector", "vv", lambda e, s_=sl, c=k: e.tensor_reduce(
                    out=acc_d[:, c:c + 1], in_=s_[:], axis=AXX, op=ALU.add,
                    apply_absolute_value=True))
                wfree[b] = ("vv", dlast)
    h_rdy_u = emit_fin(0, rlast, acc_u, "sync")
    h_rdy_g = emit_fin(1, ("vv", glast), acc_g, "sync")

    z_tok = P1(nc.sbuf_tensor("z_tok", [128, TT * D], BF16))
    z_tmp = P1(nc.sbuf_tensor("z_tmp", [128, 2 * T], F32))
    s_stage = P1(nc.sbuf_tensor("s_stage", [128, 2 * D], BF16))

    v_ztok = 0
    v_zh = [0, 0]
    ztf = [("tv", 0), ("tv", 0)]
    W("tensor", ("vv", v_rbf))
    for c in range(DC):
        zb = c % 2
        v_z = 0
        for hh in range(2):
            if v_zh[hh]:
                W("tensor", ("vv", v_zh[hh]))
            v_y = OP("tensor", "tv", lambda e, h=hh, cc=c: e.matmul(
                ps_y[:, 512 * h:512 * (h + 1)],
                gf[:, 128 * cc:128 * (cc + 1)], rbf[:, 512 * h:512 * (h + 1)],
                start=True, stop=True))
            W("vector", ("tv", v_y), ("vv", v_dp), *( [ztf[zb]] if hh == 0 else []))
            v_z = OP("vector", "vv", lambda e, cc=c, b=zb, h=hh: e.scalar_tensor_tensor(
                out=z_tmp[:, T * b + 512 * h:T * b + 512 * (h + 1)],
                in0=u_fm[:, T * cc + 512 * h:T * cc + 512 * (h + 1)],
                scalar=dp[:, cc:cc + 1], in1=ps_y[:, 512 * h:512 * (h + 1)],
                op0=ALU.mult, op1=ALU.add))
            v_zh[hh] = v_z
        W("tensor", ("vv", v_z), *( [("vv", v_ztok)] if v_ztok else []))
        for t_i in range(TT):
            v_tp = OP("tensor", "tv", lambda e, b=zb, t=t_i: e.transpose(
                ps_t[:, 128 * t:128 * (t + 1)],
                z_tmp[:, T * b + 128 * t:T * b + 128 * (t + 1)], ident[:]))
        ztf[zb] = ("tv", v_tp)
        W("vector", ("tv", v_tp))
        v_ztok = OP("vector", "vv", lambda e, cc=c: e.tensor_copy(
            out=z_tok[:].rearrange("p (t x) -> p t x", x=D)
            [:, 0:8, 128 * cc:128 * (cc + 1)],
            in_=ps_t[:].rearrange("p (t x) -> p t x", x=128)))

    W("vector", ("vv", v_ztok))
    v_mv = 0
    aggr_v = [0, 0]
    for t in range(TT):
        sl = t % 2
        if aggr_v[sl]:
            W("vector", ("vv", aggr_v[sl]))
        v_bn = 0
        for j in range(4):
            v_bn = OP("vector", "vv", lambda e, t_=t, j_=j, s_=sl: e.bn_stats(
                out=bnst[:, 24 * s_ + 6 * j_:24 * s_ + 6 * (j_ + 1)],
                in_=z_tok[:, D * t_ + 512 * j_:D * t_ + 512 * (j_ + 1)]))
        W("vector", ("vv", v_bn))
        v_mv = OP("vector", "vv", lambda e, t_=t, s_=sl: e.bn_aggr(
            out=mv[:, 2 * t_:2 * (t_ + 1)], in_=bnst[:, 24 * s_:24 * (s_ + 1)]))
        aggr_v[sl] = v_mv
    W("vector", ("vv", v_mv))
    v_ve = OP("vector", "vv", lambda e: e.tensor_scalar(
        out=sqv[:], in0=mv[:, 1:2 * TT:2], scalar1=EPS, scalar2=None, op0=ALU.add))
    W("scalar", ("vv", v_ve))
    v_sq = OP("scalar", "av", lambda e: e.activation(
        out=sqv[:], in_=sqv[:], func=AF.Sqrt))
    W("vector", ("av", v_sq))
    v_rs = OP("vector", "vv", lambda e: e.reciprocal(out=rstd[:], in_=sqv[:]))
    W("vector", ("vv", v_rs))
    sq = [0, 0]
    for t in range(TT):
        sb = t % 2
        W("vector", (f"ssc{sb}", sq[sb]))
        v_ap = OP("vector", "vv", lambda e, t_=t, b=sb: e.tensor_scalar(
            out=s_stage[:, D * b:D * (b + 1)], in0=z_tok[:, D * t_:D * (t_ + 1)],
            scalar1=mv[:, 2 * t_:2 * t_ + 1], scalar2=rstd[:, t_:t_ + 1],
            op0=ALU.subtract, op1=ALU.mult))
        W("scalar", ("vv", v_ap))
        sq[sb] = DMA("scalar", f"ssc{sb}", s_sc[128 * t:128 * (t + 1), :],
                     s_stage[:, D * sb:D * (sb + 1)])

    p1bar = {k: cnt.get(k, 0) for k in sems}

    v_sfm = 0
    W("scalar", ("ssc0", sq[0]), ("ssc1", sq[1]))
    for c in range(DC):
        v_sfm = DMAT("scalar", "sfm", s_fm[:, T * c:T * (c + 1)],
                     s_sc[0:T, 128 * c:128 * (c + 1)])

    psm.close()
    p1.close()

    ffn = ExitStack()
    FN = ffn.enter_context
    hid = [FN(nc.sbuf_tensor("hid0", [128, BEC * T], BF16))] * 2
    yac = FN(nc.sbuf_tensor("yac", [128, DC * T], BF16))
    sil_t = [FN(nc.sbuf_tensor(f"sil{i}", [128, T], BF16)) for i in range(2)]
    wdn = [FN(nc.sbuf_tensor(f"wdn{i}", [128, BEC * 256], BF16)) for i in range(2)]
    supsc = ExitStack()
    wupT = supsc.enter_context(nc.sbuf_tensor("wupT", [128, 2 * DC * SUPE], BF16))
    wgT = supsc.enter_context(nc.sbuf_tensor("wgT", [128, 2 * DC * SUPE], BF16))
    ps_a = [FN(nc.psum_tensor(f"ps_a{i}", [128, T], F32)) for i in range(4)]

    for eng in ("tensor", "vector", "scalar", "sync", "gpsimd"):
        W(eng, *[(k, v) for k, v in p1bar.items() if v])

    scr2_q = [(("qw0", 0), ("qw1", 0))] * NSLAB
    sup_ready = [((None, 0), (None, 0))] * NSUP
    mm_sup_done = [0] * NSUP
    u_free = [vfree[0], vfree[1], vfree[2], ("vv", 0), ("vv", 0), ("vv", 0)]
    d_free = [wfree[0], wfree[1]]
    dq = [0, 0]
    d_tq = [("qw0", 0), ("qw1", 0)]
    pair_last = {0: 0, 1: 0}
    state = {"uq": [0, 0, 0], "gq": [0, 0], "dq": [0, 0]}

    ldsem = ["tw0", "tw1", "tw2", "dt0", "dt1", "dt2"]
    ldval = [0] * 6

    def emit_ug_load(q, c, mat):
        idx = 2 * (DC * q + c) + mat
        b = idx % 6
        src_ap = (wu_in if mat == 0 else wg_in)[128 * c:128 * (c + 1),
                                                1024 * q:1024 * (q + 1)]
        W("gpsimd", u_free[b])
        ldval[b] = DMA("gpsimd", ldsem[b], wst[b][:], src_ap)

    def emit_ug_quant(q, c, mat):
        idx = 2 * (DC * q + c) + mat
        li = idx - 32 * q
        b = idx % 6
        dT = wupT if mat == 0 else wgT
        dst = dT[:].rearrange("p (s x) -> p s x", s=2)[:, :, SUPE * c:SUPE * (c + 1)]
        hp = h_u if mat == 0 else h_g
        hn = nh_u if mat == 0 else nh_g
        hrd = h_rdy_u if mat == 0 else h_rdy_g
        W("vector", (ldsem[b], ldval[b]), hrd,
          *( [("tv", mm_sup_done[2 * q - 1])] if q >= 1 and li < 6 else []))
        rg_ = OP("vector", "vv", lambda e, bb=b, d_=dst, hh_=hp: e.tensor_scalar(
            out=d_, in0=wst[bb][:].rearrange("p (s x) -> p s x", s=2),
            scalar1=hh_, scalar2=None, op0=ALU.is_gt))
        W("vector", ("vv", rg_))
        r = OP("vector", "vv", lambda e, bb=b, d_=dst, hh_=hn: e.scalar_tensor_tensor(
            out=d_, in0=wst[bb][:].rearrange("p (s x) -> p s x", s=2),
            scalar=hh_, in1=d_, op0=ALU.is_lt, op1=ALU.subtract))
        u_free[b] = ("vv", r)
        pair_last[mat] = r
        if li == 31:
            sup_ready[2 * q] = sup_ready[2 * q + 1] = (
                ("vv", pair_last[0]), ("vv", pair_last[1]))

    def emit_one_down_tile(i, h_rdy_d):
        b = i % 2
        W("gpsimd", d_free[b])
        state["dq"][b] = DMA("gpsimd", f"qd{b}", wstd[b][:],
                             wd_in[128 * i:128 * (i + 1), :])
        W("vector", (f"qd{b}", state["dq"][b]), d_tq[b], h_rdy_d)
        rg_ = OP("vector", "vv", lambda e, bb=b: e.tensor_scalar(
            out=terd_t[bb][:], in0=wstd[bb][:], scalar1=h_d, scalar2=None,
            op0=ALU.is_gt))
        W("vector", ("vv", rg_))
        r = OP("vector", "vv", lambda e, bb=b: e.scalar_tensor_tensor(
            out=terd_t[bb][:], in0=wstd[bb][:], scalar=nh_d, in1=terd_t[bb][:],
            op0=ALU.is_lt, op1=ALU.subtract))
        d_free[b] = ("vv", r)
        W("gpsimd", ("vv", r))
        w = DMA("gpsimd", f"qw{b}", scr2_d[128 * i:128 * (i + 1), :], terd_t[b][:])
        d_tq[b] = (f"qw{b}", w)

    v_hid = [0] * (EC + 2)
    state_v = {"silu": 0, "flush": 0, "z2tok": 0}
    flush_v = [0] * DC
    dn_free = [("tv", 0)] * 2
    h_rdy_d_v = [("vv", 0)]

    def emit_A_block(b, first_waits, steps):
        si_extra = [0, 0]
        for si in range(2):
            s = 2 * b + si
            W("tensor", sup_ready[s][0], sup_ready[s][1],
              *(first_waits if si == 0 else []))
            for ci in range(SUPC):
                ec = SUPC * s + ci
                ecb = ec % BEC
                pa = ps_a[2 * (ec % 2)]
                pg = ps_a[2 * (ec % 2) + 1]
                if ec >= 2:
                    W("tensor", ("vv", v_hid[ec - 2]))
                v_mmu = v_mmg = 0
                for hh in range(2):
                    for c in range(DC):
                        fn = lambda e, p_=pa, cc=c, h=hh, s_=s, ci_=ci: e.matmul(
                            p_[:, 512 * h:512 * (h + 1)],
                            wupT[:, 8192 * (s_ % 2) + SUPE * cc + 128 * ci_:
                                 8192 * (s_ % 2) + SUPE * cc + 128 * (ci_ + 1)],
                            s_fm[:, T * cc + 512 * h:T * cc + 512 * (h + 1)],
                            start=(cc == 0), stop=(cc == 15))
                        if hh == 1 and c == 15:
                            v_mmu = OP("tensor", "tv", fn)
                        else:
                            OPq("tensor", fn)
                for hh in range(2):
                    for c in range(DC):
                        fn = lambda e, p_=pg, cc=c, h=hh, s_=s, ci_=ci: e.matmul(
                            p_[:, 512 * h:512 * (h + 1)],
                            wgT[:, 8192 * (s_ % 2) + SUPE * cc + 128 * ci_:
                                8192 * (s_ % 2) + SUPE * cc + 128 * (ci_ + 1)],
                            s_fm[:, T * cc + 512 * h:T * cc + 512 * (h + 1)],
                            start=(cc == 0), stop=(cc == 15))
                        if hh == 1 and c == 15:
                            v_mmg = OP("tensor", "tv", fn)
                        else:
                            OPq("tensor", fn)
                mm_sup_done[s] = v_mmg
                sb2 = ec % 2
                W("scalar", ("tv", v_mmu), h_rdy_u,
                  *( [("vv", v_hid[ec - 2])] if ec >= 2 else []))
                state_v["silu"] = OP("scalar", "av", lambda e, p_=pa, i=sb2: e.activation(
                    out=sil_t[i][:], in_=p_[:], func=AF.Silu, scale=nsu))
                W("vector", ("av", state_v["silu"]), ("tv", v_mmg), h_rdy_g)
                v_hid[ec] = OP("vector", "vv",
                               lambda e, p_=pg, i=sb2, eb=ecb, hb=b % 2:
                               e.scalar_tensor_tensor(
                                   out=hid[hb][:, T * eb:T * (eb + 1)], in0=p_[:],
                                   scalar=nsg, in1=sil_t[i][:],
                                   op0=ALU.mult, op1=ALU.mult))
                for _ in range(2):
                    if steps:
                        steps.pop(0)()
        while steps:
            steps.pop(0)()

    def emit_B_block(b, z2_parts, ug_steps):
        dnq = [0, 0]
        z2prev = ("vv", 0)
        for c in range(DC):
            g4 = c // 2
            db = g4 % 2
            pd = ps_a[2 * (c % 2)]
            if c % 2 == 0:
                W("sync", scr2_q[b][0], scr2_q[b][1], dn_free[db])
                dnq[db] = DMA("sync", f"wd{db}",
                              wdn[db][:].rearrange("p (k j) -> p k j", j=256),
                              scr2_d[1024 * b:1024 * (b + 1), 256 * g4:256 * (g4 + 1)]
                              .rearrange("(k p) j -> p k j", p=128))
            W("tensor", (f"wd{db}", dnq[db]), ("vv", v_hid[BEC * b + BEC - 1]))
            if c >= 2:
                W("tensor", ("vv", state_v["flush"]))
            else:
                W("tensor", ("av", state_v["silu"]))
            v_mmd = 0
            for hh in range(2):
                for k in range(BEC):
                    fn = lambda e, p_=pd, kk=k, h=hh, d_=db, j_=c % 2, hb=b % 2: e.matmul(
                        p_[:, 512 * h:512 * (h + 1)],
                        wdn[d_][:, 256 * kk + 128 * j_:256 * kk + 128 * (j_ + 1)],
                        hid[hb][:, T * kk + 512 * h:T * kk + 512 * (h + 1)],
                        start=(kk == 0), stop=(kk == BEC - 1))
                    if hh == 1 and k == BEC - 1:
                        v_mmd = OP("tensor", "tv", fn)
                    else:
                        OPq("tensor", fn)
            if c % 2 == 1:
                dn_free[db] = ("tv", v_mmd)
            if b == 0:
                W("vector", ("tv", v_mmd), h_rdy_d_v[0])
                flush_v[c] = OP("vector", "vv",
                                lambda e, p_=pd, cc=c: e.scalar_tensor_tensor(
                                    out=yac[:, T * cc:T * (cc + 1)], in0=p_[:],
                                    scalar=nsd, in1=s_fm[:, T * cc:T * (cc + 1)],
                                    op0=ALU.mult, op1=ALU.add))
                state_v["flush"] = flush_v[c]
                for _ in range(2):
                    if ug_steps:
                        ug_steps.pop(0)()
            elif b < NB - 1:
                W("vector", ("tv", v_mmd), ("vv", flush_v[c]))
                flush_v[c] = OP("vector", "vv",
                                lambda e, p_=pd, cc=c: e.scalar_tensor_tensor(
                                    out=yac[:, T * cc:T * (cc + 1)], in0=p_[:],
                                    scalar=nsd, in1=yac[:, T * cc:T * (cc + 1)],
                                    op0=ALU.mult, op1=ALU.add))
                state_v["flush"] = flush_v[c]
                for _ in range(2):
                    if ug_steps:
                        ug_steps.pop(0)()
            else:
                z2f, z2tok = z2_parts
                W("vector", ("tv", v_mmd), ("vv", flush_v[c]), z2prev)
                v_z2 = OP("vector", "vv", lambda e, p_=pd, cc=c: e.scalar_tensor_tensor(
                    out=z2f[:], in0=p_[:], scalar=nsd,
                    in1=yac[:, T * cc:T * (cc + 1)], op0=ALU.mult, op1=ALU.add))
                W("tensor", ("vv", v_z2))
                for half in range(2):
                    pt_ = ps_a[1 + 2 * half]
                    for j in range(4):
                        t_i = 4 * half + j
                        v_tp2 = OP("tensor", "tv", lambda e, p_=pt_, t=t_i: e.transpose(
                            p_[:, 128 * (t % 4):128 * (t % 4 + 1)],
                            z2f[:, 128 * t:128 * (t + 1)], ident[:]))
                    W("vector", ("tv", v_tp2))
                    vzt = OP("vector", "vv", lambda e, p_=pt_, cc=c, hf=half: e.tensor_copy(
                        out=z2tok[:].rearrange("p (t x) -> p t x", x=D)
                        [:, 4 * hf:4 * (hf + 1), 128 * cc:128 * (cc + 1)],
                        in_=p_[:, 0:512].rearrange("p (t x) -> p t x", x=128)))
                z2prev = ("vv", vzt)
                state_v["flush"] = vzt
                state_v["z2tok"] = vzt
                for _ in range(2):
                    if ug_steps:
                        ug_steps.pop(0)()
        while ug_steps:
            ug_steps.pop(0)()

    ptiles = [(0, cc, mat) for cc in range(DC) for mat in range(2)]
    for t6 in ptiles[:6]:
        emit_ug_load(*t6)
    for ki, t in enumerate(ptiles):
        emit_ug_quant(*t)
        if ki + 6 < 32:
            emit_ug_load(*ptiles[ki + 6])

    first_A = [("sfm", v_sfm), ("misc", v_ident)]
    z2_parts = None
    for b in range(NB):
        tiles = ([(b + 1, cc, mat) for cc in range(DC) for mat in range(2)]
                 if b + 1 < NB else [])
        steps = []
        for t6 in tiles[:6]:
            steps.append(lambda t=t6: emit_ug_load(*t))
        if b == 0:
            def fin2_step():
                h_rdy_d_v[0] = emit_fin(2, ("vv", dlast), acc_d, "gpsimd")
            steps.append(fin2_step)
            for m_ in range(3):
                for k_ in range(BEC):
                    steps.append(lambda i_=BEC * m_ + k_:
                                 emit_one_down_tile(i_, h_rdy_d_v[0]))
                steps.append(lambda m=m_: scr2_q.__setitem__(
                    m, (d_tq[0], d_tq[1])))
        if b >= 1 and b + 2 < NSLAB:
            for k in range(BEC):
                steps.append(lambda i_=BEC * (b + 2) + k:
                             emit_one_down_tile(i_, h_rdy_d_v[0]))
        ug_steps = []

        def mk_step(ki, t, tl):
            def f():
                emit_ug_quant(*t)
                if ki + 6 < len(tl):
                    emit_ug_load(*tl[ki + 6])
            return f
        for ki, t in enumerate(tiles):
            ug_steps.append(mk_step(ki, t, tiles))
        emit_A_block(b, first_A if b == 0 else
                     [("vv", state_v["flush"]), ("av", state_v["silu"])], steps)
        if b + 2 < NSLAB and b >= 1:
            scr2_q[b + 2] = (d_tq[0], d_tq[1])
        if b == NB - 1:
            supsc.close()
            z2f = FN(nc.sbuf_tensor("z2f", [128, T], F32))
            z2tok = FN(nc.sbuf_tensor("z2tok", [128, TT * D], BF16))
            ostage = FN(nc.sbuf_tensor("ost", [128, D], F32))
            z2_parts = (z2f, z2tok)
            emit_B_block(b, z2_parts, ug_steps)
        else:
            emit_B_block(b, None, ug_steps)

    W("vector", ("vv", state_v["z2tok"]))
    v_mv2 = 0
    aggr_v = [0, 0]
    for t in range(TT):
        sl = t % 2
        if aggr_v[sl]:
            W("vector", ("vv", aggr_v[sl]))
        v_bn2 = 0
        for j in range(4):
            v_bn2 = OP("vector", "vv", lambda e, t_=t, j_=j, s_=sl: e.bn_stats(
                out=bnst[:, 24 * s_ + 6 * j_:24 * s_ + 6 * (j_ + 1)],
                in_=z2tok[:, D * t_ + 512 * j_:D * t_ + 512 * (j_ + 1)]))
        W("vector", ("vv", v_bn2))
        v_mv2 = OP("vector", "vv", lambda e, t_=t, s_=sl: e.bn_aggr(
            out=mv[:, 2 * t_:2 * (t_ + 1)], in_=bnst[:, 24 * s_:24 * (s_ + 1)]))
        aggr_v[sl] = v_mv2
    W("vector", ("vv", v_mv2))
    v_ve2 = OP("vector", "vv", lambda e: e.tensor_scalar(
        out=sqv[:], in0=mv[:, 1:2 * TT:2], scalar1=EPS, scalar2=None, op0=ALU.add))
    W("scalar", ("vv", v_ve2))
    v_sq2 = OP("scalar", "av", lambda e: e.activation(
        out=sqv[:], in_=sqv[:], func=AF.Sqrt))
    W("vector", ("av", v_sq2))
    v_rs2 = OP("vector", "vv", lambda e: e.reciprocal(out=rstd[:], in_=sqv[:]))
    W("vector", ("vv", v_rs2))
    oq = 0
    for t in range(TT):
        W("vector", ("out0", oq))
        v_o = OP("vector", "vv", lambda e, t_=t: e.tensor_scalar(
            out=ostage[:], in0=z2tok[:, D * t_:D * (t_ + 1)],
            scalar1=mv[:, 2 * t_:2 * t_ + 1], scalar2=rstd[:, t_:t_ + 1],
            op0=ALU.subtract, op1=ALU.mult))
        W("sync", ("vv", v_o))
        oq = DMA("sync", "out0", out[128 * t:128 * (t + 1), :], ostage[:])
    W("sync", ("out0", oq))
    W("gpsimd", ("out0", oq))

    with nc.Block() as block:
        @block.sync
        def _(e):
            for f in ops["sync"]:
                f(e)

        @block.scalar
        def _(e):
            for f in ops["scalar"]:
                f(e)

        @block.vector
        def _(e):
            for f in ops["vector"]:
                f(e)

        @block.tensor
        def _(e):
            for f in ops["tensor"]:
                f(e)

        @block.gpsimd
        def _(e):
            for f in ops["gpsimd"]:
                f(e)

    ffn.close()
    es.close()
    return nc


_PT = np.zeros((8, 2, 256), np.float32)
for _lvl in range(8):
    _bits = (np.arange(256) >> (7 - _lvl)) & 1
    _PT[_lvl, 0, _bits == 0] = 1.0
    _PT[_lvl, 1, _bits == 1] = 1.0


def kernel(L0_attn_u, L0_attn_A_factors, L0_attn_W_b_rank, L0_attn_W_b_state,
           L0_attn_W_c_rank, L0_attn_W_c_model, L0_attn_D_vec,
           L0_ffn_W_up, L0_ffn_W_gate, L0_ffn_W_down):
    u = np.ascontiguousarray(np.asarray(L0_attn_u, np.float32).reshape(NCORES * T, D))
    at = np.ascontiguousarray(np.asarray(L0_attn_A_factors, np.float32).transpose(0, 2, 1))
    wbs_t = np.ascontiguousarray(np.asarray(L0_attn_W_b_state, np.float32).T)
    dvec = np.ascontiguousarray(np.asarray(L0_attn_D_vec, np.float32).reshape(DC, 128).T)

    nc = build_core()
    common = {
        "w_b_rank": np.ascontiguousarray(np.asarray(L0_attn_W_b_rank, np.float32)),
        "a_t": at, "pt_all": _PT, "w_bs_t": wbs_t,
        "w_c_rank": np.ascontiguousarray(np.asarray(L0_attn_W_c_rank, np.float32)),
        "w_c_model": np.ascontiguousarray(np.asarray(L0_attn_W_c_model, np.float32)),
        "d_vec": dvec,
        "w_up": np.ascontiguousarray(np.asarray(L0_ffn_W_up, np.float32).T),
        "w_gate": np.ascontiguousarray(np.asarray(L0_ffn_W_gate, np.float32).T),
        "w_down": np.ascontiguousarray(np.asarray(L0_ffn_W_down, np.float32).T),
        "ident": np.eye(128, dtype=np.float32),
    }
    in_maps = [dict(common, u=np.ascontiguousarray(u[T * i:T * (i + 1)]))
               for i in range(NCORES)]
    res = run_bass_kernel_spmd(nc, in_maps, core_ids=list(range(NCORES)))
    full = np.concatenate([np.asarray(r["out"]) for r in res.results], 0)
    return full.reshape(2, 4096, 2048).astype(np.float32)


# version 29
# speedup vs baseline: 1.1474x; 1.0632x over previous
import numpy as np
import concourse.bass as bass
import concourse.mybir as mybir
from concourse.bass_utils import run_bass_kernel_spmd
from concourse.alu_op_type import AluOpType as ALU
from contextlib import ExitStack

F32, BF16 = mybir.dt.float32, mybir.dt.bfloat16
AF = mybir.ActivationFunctionType
AXX = mybir.AxisListType.X

NCORES = 8
T = 1024
TT = 8
D = 2048
DC = 16
E = 8192
EC = 64
NB = 8
BEC = 8
NSUP = 16
SUPE = 512
SUPC = 4
NSLAB = 8
EPS = 1e-5


def build_core():
    nc = bass.Bass("TRN2", target_bir_lowering=False, debug=False)

    u_in = nc.dram_tensor("u", [T, D], F32, kind="ExternalInput").ap()
    wb_in = nc.dram_tensor("w_b_rank", [D, 64], F32, kind="ExternalInput").ap()
    at_in = nc.dram_tensor("a_t", [8, 2, 2], F32, kind="ExternalInput").ap()
    pt_in = nc.dram_tensor("pt_all", [8, 2, 256], F32, kind="ExternalInput").ap()
    wbsT_in = nc.dram_tensor("w_bs_t", [256, 64], F32, kind="ExternalInput").ap()
    wcr_in = nc.dram_tensor("w_c_rank", [256, 64], F32, kind="ExternalInput").ap()
    wcm_in = nc.dram_tensor("w_c_model", [64, D], F32, kind="ExternalInput").ap()
    dvec_in = nc.dram_tensor("d_vec", [128, DC], F32, kind="ExternalInput").ap()
    wu_in = nc.dram_tensor("w_up", [D, E], F32, kind="ExternalInput").ap()
    wg_in = nc.dram_tensor("w_gate", [D, E], F32, kind="ExternalInput").ap()
    wd_in = nc.dram_tensor("w_down", [E, D], F32, kind="ExternalInput").ap()
    ident_in = nc.dram_tensor("ident", [128, 128], F32, kind="ExternalInput").ap()
    out = nc.dram_tensor("out", [T, D], F32, kind="ExternalOutput").ap()

    u_bf = nc.dram_tensor("u_bf_scr", [T, D], BF16)
    s_sc = nc.dram_tensor("s_scr", [T, D], BF16)
    scr2_d = nc.dram_tensor("scr2_d_scr", [E, D], BF16)
    scrh = [nc.dram_tensor(f"scrh{m}", [128, 1], F32) for m in range(3)]
    scrh2 = [nc.dram_tensor(f"scrh2_{m}", [1, 3], F32) for m in range(3)]

    ops = {e: [] for e in ("sync", "scalar", "vector", "tensor", "gpsimd")}
    cnt = {}
    es = ExitStack()
    EN = es.enter_context
    semnames = (["tv", "vv", "av", "misc", "ubf", "ufm", "wbq", "ssc0", "ssc1",
                 "sfm", "out0", "fin0", "fin1", "fin2", "msc2", "msc3",
                 "dtw0", "dtw1"]
                + [f"{p}{i}" for p in ("ldd", "lgg", "lgd", "tdw", "dt",
                                       "trU", "trG", "qd", "qw", "s2g") for i in range(3)]
                + [f"{p}{i}" for p in ("ld", "tw") for i in range(3)]
                + [f"wd{i}" for i in range(4)])
    sems = {n: EN(nc.semaphore(n)) for n in semnames}

    def W(eng, *deps):
        for sname, v in deps:
            if v and v > 0:
                ops[eng].append(lambda e, s=sems[sname], v_=v: e.wait_ge(s, v_))

    def DMA(eng, qname, outap, inap, **kw):
        cnt[qname] = cnt.get(qname, 0) + 16
        ops[eng].append(lambda e, o=outap, i=inap, s=sems[qname], kw_=kw:
                        e.dma_start(out=o, in_=i, **kw_).then_inc(s, 16))
        return cnt[qname]

    def DMAT(eng, qname, outap, inap):
        cnt[qname] = cnt.get(qname, 0) + 16
        ops[eng].append(lambda e, o=outap, i=inap, s=sems[qname]:
                        e.dma_start_transpose(out=o, in_=i).then_inc(s, 16))
        return cnt[qname]

    def OP(eng, sname, fn):
        cnt[sname] = cnt.get(sname, 0) + 1
        ops[eng].append(lambda e, f=fn, s=sems[sname]: f(e).then_inc(s, 1))
        return cnt[sname]

    def OPq(eng, fn):
        ops[eng].append(lambda e, f=fn: f(e))

    s_fm = EN(nc.sbuf_tensor("s_fm", [128, DC * T], BF16))
    ident = EN(nc.sbuf_tensor("ident_s", [128, 128], F32))
    hc = EN(nc.sbuf_tensor("hc", [128, 12], F32))
    hrow = EN(nc.sbuf_tensor("hrow", [1, 16], F32))
    rowb = EN(nc.sbuf_tensor("rowb", [1, 3 * 128], F32))
    acc_u = EN(nc.sbuf_tensor("acc_u", [128, 2 * EC], F32))
    acc_g = EN(nc.sbuf_tensor("acc_g", [128, EC], F32))
    acc_d = EN(nc.sbuf_tensor("acc_d", [128, EC], F32))
    accr = EN(nc.sbuf_tensor("accr", [128, 4], F32))
    bnst = EN(nc.sbuf_tensor("bnst", [128, 2 * 24], F32))
    mv = EN(nc.sbuf_tensor("mv", [128, TT * 2], F32))
    rstd = EN(nc.sbuf_tensor("rstd", [128, TT], F32))
    sqv = EN(nc.sbuf_tensor("sqv", [128, TT], F32))
    wst = [EN(nc.sbuf_tensor(f"wst{i}", [128, 1024], F32)) for i in range(6)]
    wstd = [EN(nc.sbuf_tensor(f"wstd{i}", [128, 2048], F32)) for i in range(2)]
    terd_t = [EN(nc.sbuf_tensor(f"terd_t{i}", [128, 2048], BF16)) for i in range(2)]

    h_u, nh_u, nsu = hc[:, 0:1], hc[:, 1:2], hc[:, 2:3]
    h_g, nh_g, nsg = hc[:, 4:5], hc[:, 5:6], hc[:, 6:7]
    h_d, nh_d, nsd = hc[:, 8:9], hc[:, 9:10], hc[:, 10:11]

    v_ident = DMA("sync", "misc", ident[:], ident_in)

    p1 = ExitStack()
    P1 = p1.enter_context
    u_fm = P1(nc.sbuf_tensor("u_fm", [128, DC * T], BF16))
    wbq = P1(nc.sbuf_tensor("wbq", [128, DC * 64], BF16))
    gf = P1(nc.sbuf_tensor("gf", [64, D], BF16))
    dp = P1(nc.sbuf_tensor("dp", [128, DC], F32))
    dvc = P1(nc.sbuf_tensor("dvc", [128, DC], F32))
    rbf = P1(nc.sbuf_tensor("rbf", [64, T], BF16))
    wstd2x = P1(nc.sbuf_tensor("wstd2x", [128, 2048], F32))

    v_dvc = DMA("sync", "msc2", dvc[:], dvec_in)
    v_wb = DMA("gpsimd", "wbq", wbq[:].rearrange("p (c r) -> p c r", r=64),
               wb_in.rearrange("(c p) r -> p c r", p=128))
    W("vector", ("msc2", v_dvc))
    v_dp = OP("vector", "vv", lambda e: e.tensor_scalar(
        out=dp[:], in0=dvc[:], scalar1=1.0, scalar2=None, op0=ALU.add))
    ucst = P1(nc.sbuf_tensor("ucst", [128, 2048], F32))
    ucb = P1(nc.sbuf_tensor("ucb", [128, 2048], BF16))
    vcast = 0
    vwr = 0
    for j in range(TT):
        if vcast:
            W("scalar", ("vv", vcast))
        vu = DMA("scalar", "ubf", ucst[:], u_in[128 * j:128 * (j + 1), :])
        W("vector", ("ubf", vu), *( [("ufm", vwr)] if vwr else []))
        vcast = OP("vector", "vv", lambda e: e.tensor_copy(out=ucb[:], in_=ucst[:]))
        W("scalar", ("vv", vcast))
        vwr = DMA("scalar", "ufm", u_bf[128 * j:128 * (j + 1), :], ucb[:])
    ut_v = 0
    W("scalar", ("ufm", vwr))
    for c in range(DC):
        ut_v = DMAT("scalar", "ufm", u_fm[:, T * c:T * (c + 1)],
                    u_bf[0:T, 128 * c:128 * (c + 1)])

    fold = ExitStack()
    FE = fold.enter_context
    at_s = FE(nc.sbuf_tensor("at_s", [2, 16], F32))
    pt_s = FE(nc.sbuf_tensor("pt_s", [2, 8 * 256], F32))
    t1_s = FE(nc.sbuf_tensor("t1_s", [2, 256], F32))
    m_s = FE(nc.sbuf_tensor("m_s", [128, 512], F32))
    wbsT_s = FE(nc.sbuf_tensor("wbsT_s", [128, 2 * 64], F32))
    wcr_s = FE(nc.sbuf_tensor("wcr_s", [128, 2 * 64], F32))
    wcm_s = FE(nc.sbuf_tensor("wcm_s", [64, D], F32))
    g1_s = FE(nc.sbuf_tensor("g1_s", [64, 256], F32))
    g1t_s = FE(nc.sbuf_tensor("g1t_s", [128, 2 * 64], F32))
    g2t_s = FE(nc.sbuf_tensor("g2t_s", [64, 64], F32))
    psf = ExitStack()
    ps_f = psf.enter_context(nc.psum_tensor("ps_f", [128, 512], F32))
    ps_f2 = psf.enter_context(nc.psum_tensor("ps_f2", [64, 512], F32))

    DMA("sync", "msc3", at_s[:].rearrange("j (l i) -> j l i", i=2),
        at_in.rearrange("l j i -> j l i"))
    DMA("sync", "msc3", pt_s[:].rearrange("j (l n) -> j l n", n=256),
        pt_in.rearrange("l j n -> j l n"))
    DMA("sync", "msc3", wbsT_s[:].rearrange("p (k m) -> p k m", m=64),
        wbsT_in.rearrange("(k p) m -> p k m", p=128))
    DMA("sync", "msc3", wcr_s[:].rearrange("p (k m) -> p k m", m=64),
        wcr_in.rearrange("(k p) m -> p k m", p=128))
    ld_fold = DMA("sync", "msc3", wcm_s[:], wcm_in)

    W("tensor", ("msc3", ld_fold), ("misc", v_ident))
    v_m = 0
    for lvl in range(8):
        if v_m:
            W("tensor", ("vv", v_m))
        v_t1 = OP("tensor", "tv", lambda e, l=lvl: e.matmul(
            ps_f2[0:2, 0:256], at_s[:, 2 * l:2 * (l + 1)],
            pt_s[:, 256 * l:256 * (l + 1)], start=True, stop=True))
        W("vector", ("tv", v_t1), ("vv", v_m))
        v_t1e = OP("vector", "vv", lambda e: e.tensor_copy(
            out=t1_s[:], in_=ps_f2[0:2, 0:256]))
        W("tensor", ("vv", v_t1e))
        for ch in range(2):
            v_c = OP("tensor", "tv", lambda e, l=lvl, c=ch: e.matmul(
                ps_f[:, 256 * c:256 * (c + 1)],
                pt_s[:, 256 * l + 128 * c:256 * l + 128 * (c + 1)],
                t1_s[:], start=True, stop=True))
        W("vector", ("tv", v_c))
        if lvl == 0:
            v_m = OP("vector", "vv", lambda e: e.tensor_copy(out=m_s[:], in_=ps_f[:]))
        else:
            v_m = OP("vector", "vv", lambda e: e.tensor_tensor(
                out=m_s[:], in0=m_s[:], in1=ps_f[:], op=ALU.mult))
    W("tensor", ("vv", v_m))
    OPq("tensor", lambda e: e.matmul(
        ps_f2[0:64, 0:256], wbsT_s[:, 0:64], m_s[:, 0:256], start=True, stop=False))
    v_g1 = OP("tensor", "tv", lambda e: e.matmul(
        ps_f2[0:64, 0:256], wbsT_s[:, 64:128], m_s[:, 256:512], start=False, stop=True))
    W("vector", ("tv", v_g1))
    v_g1e = OP("vector", "vv", lambda e: e.tensor_copy(
        out=g1_s[:], in_=ps_f2[0:64, 0:256]))
    W("tensor", ("vv", v_g1e))
    for k in range(2):
        v_tp = OP("tensor", "tv", lambda e, kk=k: e.transpose(
            ps_f[:, 64 * kk:64 * (kk + 1)], g1_s[:, 128 * kk:128 * (kk + 1)],
            ident[0:64, 0:64]))
    W("vector", ("tv", v_tp))
    v_g1t = OP("vector", "vv", lambda e: e.tensor_copy(
        out=g1t_s[:], in_=ps_f[:, 0:128]))
    W("tensor", ("vv", v_g1t))
    OPq("tensor", lambda e: e.matmul(
        ps_f2[0:64, 0:64], wcr_s[:, 0:64], g1t_s[:, 0:64], start=True, stop=False))
    v_g2 = OP("tensor", "tv", lambda e: e.matmul(
        ps_f2[0:64, 0:64], wcr_s[:, 64:128], g1t_s[:, 64:128], start=False, stop=True))
    W("vector", ("tv", v_g2))
    v_g2e = OP("vector", "vv", lambda e: e.tensor_copy(
        out=g2t_s[:], in_=ps_f2[0:64, 0:64]))
    W("tensor", ("vv", v_g2e))
    v_gfe = 0
    for k in range(4):
        if v_gfe:
            W("tensor", ("vv", v_gfe))
        v_gf = OP("tensor", "tv", lambda e, kk=k: e.matmul(
            ps_f[0:64, 0:512], g2t_s[:], wcm_s[:, 512 * kk:512 * (kk + 1)],
            start=True, stop=True))
        W("vector", ("tv", v_gf))
        v_gfe = OP("vector", "vv", lambda e, kk=k: e.tensor_copy(
            out=gf[:, 512 * kk:512 * (kk + 1)], in_=ps_f[0:64, 0:512]))
    psf.close()
    fold.close()

    def emit_fin(m, vdep, acc, q_eng):
        fsem = f"fin{m}"
        W("vector", vdep if isinstance(vdep, tuple) else ("vv", vdep))
        r = OP("vector", "vv", lambda e, a=acc, mm=m: e.tensor_reduce(
            out=accr[:, mm:mm + 1], in_=a[:], axis=AXX, op=ALU.add))
        W(q_eng, ("vv", r))
        w1 = DMA(q_eng, fsem, scrh[m].ap(), accr[:, m:m + 1])
        W(q_eng, (fsem, w1))
        w2 = DMA(q_eng, fsem, rowb[:, 128 * m:128 * (m + 1)],
                 scrh[m].ap().rearrange("(a p) b -> a (p b)", a=1))
        W("vector", (fsem, w2))
        r = OP("vector", "vv", lambda e, mm=m: e.tensor_reduce(
            out=hrow[:, 4 * mm:4 * mm + 1], in_=rowb[:, 128 * mm:128 * (mm + 1)],
            axis=AXX, op=ALU.add))
        W("vector", ("vv", r))
        r = OP("vector", "vv", lambda e, mm=m: e.tensor_scalar(
            out=hrow[:, 4 * mm:4 * mm + 1], in0=hrow[:, 4 * mm:4 * mm + 1],
            scalar1=2.0 ** -25, scalar2=5e-9, op0=ALU.mult, op1=ALU.add))
        W("vector", ("vv", r))
        r = OP("vector", "vv", lambda e, mm=m: e.tensor_scalar(
            out=hrow[:, 4 * mm + 1:4 * mm + 2], in0=hrow[:, 4 * mm:4 * mm + 1],
            scalar1=-1.0, scalar2=None, op0=ALU.mult))
        W("vector", ("vv", r))
        r = OP("vector", "vv", lambda e, mm=m: e.tensor_scalar(
            out=hrow[:, 4 * mm + 2:4 * mm + 3], in0=hrow[:, 4 * mm:4 * mm + 1],
            scalar1=-2.0, scalar2=None, op0=ALU.mult))
        W(q_eng, ("vv", r))
        w3 = DMA(q_eng, fsem, scrh2[m].ap(), hrow[:, 4 * m:4 * m + 3])
        W(q_eng, (fsem, w3))
        w4 = DMA(q_eng, fsem, hc[:, 4 * m:4 * m + 3],
                 scrh2[m].ap().squeeze(0).partition_broadcast(128))
        return (fsem, w4)

    psm = ExitStack()
    ps_r = psm.enter_context(nc.psum_tensor("ps_r", [64, T], F32))
    ps_y = psm.enter_context(nc.psum_tensor("ps_y", [128, T], F32))
    ps_t = psm.enter_context(nc.psum_tensor("ps_t", [128, T], F32))
    W("tensor", ("ufm", ut_v), ("wbq", v_wb), ("vv", v_gfe))
    v_r = 0
    for hh in range(2):
        for c in range(DC):
            fn = lambda e, h=hh, cc=c: e.matmul(
                ps_r[0:64, 512 * h:512 * (h + 1)], wbq[:, 64 * cc:64 * (cc + 1)],
                u_fm[:, T * cc + 512 * h:T * cc + 512 * (h + 1)],
                start=(cc == 0), stop=(cc == 15))
            if hh == 1 and c == 15:
                v_r = OP("tensor", "tv", fn)
            else:
                OPq("tensor", fn)
    W("vector", ("tv", v_r))
    v_rbf = OP("vector", "vv", lambda e: e.tensor_copy(out=rbf[:], in_=ps_r[0:64, :]))

    vfree = [("vv", 0), ("vv", 0), ("vv", 0)]
    vq = [0, 0, 0]
    wfree = [("vv", 0), ("vv", 0), ("vv", 0)]
    wq3 = [0, 0, 0]
    wsl = [wstd[0], wstd[1], wstd2x]
    rlast = 0
    glast = 0
    dlast = 0
    for k in range(EC):
        for half in range(2):
            ec = 2 * k + half
            b = ec % 3
            W("sync", vfree[b])
            vq[b] = DMA("sync", f"ld{b}", wst[b][:],
                        wu_in[128 * (ec // 8):128 * (ec // 8 + 1),
                              1024 * (ec % 8):1024 * (ec % 8 + 1)])
            W("vector", (f"ld{b}", vq[b]))
            rlast = OP("vector", "vv", lambda e, bb=b, c=ec: e.tensor_reduce(
                out=acc_u[:, c:c + 1], in_=wst[bb][:], axis=AXX, op=ALU.add,
                apply_absolute_value=True))
            vfree[b] = ("vv", rlast)
        for kind in range(2):
            j = 2 * k + kind
            b = j % 3
            sl = wsl[b]
            sn = (f"lgg{b}" if kind == 0 else f"lgd{b}")
            if kind == 0:
                W("gpsimd", wfree[b])
                wq3[b] = DMA("gpsimd", sn, sl[:],
                             wg_in[128 * (k // 4):128 * (k // 4 + 1),
                                   2048 * (k % 4):2048 * (k % 4 + 1)])
                W("vector", (sn, wq3[b]))
                glast = OP("vector", "vv", lambda e, s_=sl, c=k: e.tensor_reduce(
                    out=acc_g[:, c:c + 1], in_=s_[:], axis=AXX, op=ALU.add,
                    apply_absolute_value=True))
                wfree[b] = ("vv", glast)
            else:
                W("scalar", wfree[b])
                wq3[b] = DMA("scalar", sn, sl[:], wd_in[128 * k:128 * (k + 1), :])
                W("vector", (sn, wq3[b]))
                dlast = OP("vector", "vv", lambda e, s_=sl, c=k: e.tensor_reduce(
                    out=acc_d[:, c:c + 1], in_=s_[:], axis=AXX, op=ALU.add,
                    apply_absolute_value=True))
                wfree[b] = ("vv", dlast)
    h_rdy_u = emit_fin(0, rlast, acc_u, "sync")
    h_rdy_g = emit_fin(1, ("vv", glast), acc_g, "sync")

    z_tok = P1(nc.sbuf_tensor("z_tok", [128, TT * D], BF16))
    z_tmp = P1(nc.sbuf_tensor("z_tmp", [128, 2 * T], F32))
    s_stage = P1(nc.sbuf_tensor("s_stage", [128, 2 * D], BF16))

    v_ztok = 0
    v_zh = [0, 0]
    ztf = [("tv", 0), ("tv", 0)]
    W("tensor", ("vv", v_rbf))
    for c in range(DC):
        zb = c % 2
        v_z = 0
        for hh in range(2):
            if v_zh[hh]:
                W("tensor", ("vv", v_zh[hh]))
            v_y = OP("tensor", "tv", lambda e, h=hh, cc=c: e.matmul(
                ps_y[:, 512 * h:512 * (h + 1)],
                gf[:, 128 * cc:128 * (cc + 1)], rbf[:, 512 * h:512 * (h + 1)],
                start=True, stop=True))
            W("vector", ("tv", v_y), ("vv", v_dp), *( [ztf[zb]] if hh == 0 else []))
            v_z = OP("vector", "vv", lambda e, cc=c, b=zb, h=hh: e.scalar_tensor_tensor(
                out=z_tmp[:, T * b + 512 * h:T * b + 512 * (h + 1)],
                in0=u_fm[:, T * cc + 512 * h:T * cc + 512 * (h + 1)],
                scalar=dp[:, cc:cc + 1], in1=ps_y[:, 512 * h:512 * (h + 1)],
                op0=ALU.mult, op1=ALU.add))
            v_zh[hh] = v_z
        W("tensor", ("vv", v_z), *( [("vv", v_ztok)] if v_ztok else []))
        for t_i in range(TT):
            v_tp = OP("tensor", "tv", lambda e, b=zb, t=t_i: e.transpose(
                ps_t[:, 128 * t:128 * (t + 1)],
                z_tmp[:, T * b + 128 * t:T * b + 128 * (t + 1)], ident[:]))
        ztf[zb] = ("tv", v_tp)
        W("vector", ("tv", v_tp))
        v_ztok = OP("vector", "vv", lambda e, cc=c: e.tensor_copy(
            out=z_tok[:].rearrange("p (t x) -> p t x", x=D)
            [:, 0:8, 128 * cc:128 * (cc + 1)],
            in_=ps_t[:].rearrange("p (t x) -> p t x", x=128)))

    W("vector", ("vv", v_ztok))
    v_mv = 0
    aggr_v = [0, 0]
    for t in range(TT):
        sl = t % 2
        if aggr_v[sl]:
            W("vector", ("vv", aggr_v[sl]))
        v_bn = 0
        for j in range(4):
            v_bn = OP("vector", "vv", lambda e, t_=t, j_=j, s_=sl: e.bn_stats(
                out=bnst[:, 24 * s_ + 6 * j_:24 * s_ + 6 * (j_ + 1)],
                in_=z_tok[:, D * t_ + 512 * j_:D * t_ + 512 * (j_ + 1)]))
        W("vector", ("vv", v_bn))
        v_mv = OP("vector", "vv", lambda e, t_=t, s_=sl: e.bn_aggr(
            out=mv[:, 2 * t_:2 * (t_ + 1)], in_=bnst[:, 24 * s_:24 * (s_ + 1)]))
        aggr_v[sl] = v_mv
    W("vector", ("vv", v_mv))
    v_ve = OP("vector", "vv", lambda e: e.tensor_scalar(
        out=sqv[:], in0=mv[:, 1:2 * TT:2], scalar1=EPS, scalar2=None, op0=ALU.add))
    W("scalar", ("vv", v_ve))
    v_sq = OP("scalar", "av", lambda e: e.activation(
        out=sqv[:], in_=sqv[:], func=AF.Sqrt))
    W("vector", ("av", v_sq))
    v_rs = OP("vector", "vv", lambda e: e.reciprocal(out=rstd[:], in_=sqv[:]))
    W("vector", ("vv", v_rs))
    sq = [0, 0]
    for t in range(TT):
        sb = t % 2
        W("vector", (f"ssc{sb}", sq[sb]))
        v_ap = OP("vector", "vv", lambda e, t_=t, b=sb: e.tensor_scalar(
            out=s_stage[:, D * b:D * (b + 1)], in0=z_tok[:, D * t_:D * (t_ + 1)],
            scalar1=mv[:, 2 * t_:2 * t_ + 1], scalar2=rstd[:, t_:t_ + 1],
            op0=ALU.subtract, op1=ALU.mult))
        W("scalar", ("vv", v_ap))
        sq[sb] = DMA("scalar", f"ssc{sb}", s_sc[128 * t:128 * (t + 1), :],
                     s_stage[:, D * sb:D * (sb + 1)])

    p1bar = {k: cnt.get(k, 0) for k in sems}

    v_sfm = 0
    W("scalar", ("ssc0", sq[0]), ("ssc1", sq[1]))
    for c in range(DC):
        v_sfm = DMAT("scalar", "sfm", s_fm[:, T * c:T * (c + 1)],
                     s_sc[0:T, 128 * c:128 * (c + 1)])

    psm.close()
    p1.close()

    ffn = ExitStack()
    FN = ffn.enter_context
    hid = [FN(nc.sbuf_tensor("hid0", [128, BEC * T], BF16))] * 2
    yac = FN(nc.sbuf_tensor("yac", [128, DC * T], BF16))
    sil_t = [FN(nc.sbuf_tensor(f"sil{i}", [128, T], BF16)) for i in range(2)]
    wdn = [FN(nc.sbuf_tensor(f"wdn{i}", [128, BEC * 256], BF16)) for i in range(2)]
    supsc = ExitStack()
    wupT = supsc.enter_context(nc.sbuf_tensor("wupT", [128, 2 * DC * SUPE], BF16))
    wgT = supsc.enter_context(nc.sbuf_tensor("wgT", [128, 2 * DC * SUPE], BF16))
    ps_a = [FN(nc.psum_tensor(f"ps_a{i}", [128, T], F32)) for i in range(4)]

    for eng in ("tensor", "vector", "scalar", "sync", "gpsimd"):
        W(eng, *[(k, v) for k, v in p1bar.items() if v])

    scr2_q = [(("qw0", 0), ("qw1", 0))] * NSLAB
    sup_ready = [((None, 0), (None, 0))] * NSUP
    mm_sup_done = [0] * NSUP
    u_free = [vfree[0], vfree[1], vfree[2], ("vv", 0), ("vv", 0), ("vv", 0)]
    d_free = [wfree[0], wfree[1]]
    dq = [0, 0]
    d_tq = [("qw0", 0), ("qw1", 0)]
    pair_last = {0: 0, 1: 0}
    state = {"uq": [0, 0, 0], "gq": [0, 0], "dq": [0, 0]}

    ldsem = ["tw0", "tw1", "tw2", "dt0", "dt1", "dt2"]
    ldval = [0] * 6

    def emit_ug_load(q, c, mat):
        idx = 2 * (DC * q + c) + mat
        b = idx % 6
        src_ap = (wu_in if mat == 0 else wg_in)[128 * c:128 * (c + 1),
                                                1024 * q:1024 * (q + 1)]
        W("gpsimd", u_free[b])
        ldval[b] = DMA("gpsimd", ldsem[b], wst[b][:], src_ap)

    def emit_ug_quant(q, c, mat):
        idx = 2 * (DC * q + c) + mat
        li = idx - 32 * q
        b = idx % 6
        dT = wupT if mat == 0 else wgT
        dst = dT[:].rearrange("p (s x) -> p s x", s=2)[:, :, SUPE * c:SUPE * (c + 1)]
        hp = h_u if mat == 0 else h_g
        hn = nh_u if mat == 0 else nh_g
        hrd = h_rdy_u if mat == 0 else h_rdy_g
        W("vector", (ldsem[b], ldval[b]), hrd,
          *( [("tv", mm_sup_done[2 * q - 1])] if q >= 1 and li < 6 else []))
        rg_ = OP("vector", "vv", lambda e, bb=b, d_=dst, hh_=hp: e.tensor_scalar(
            out=d_, in0=wst[bb][:].rearrange("p (s x) -> p s x", s=2),
            scalar1=hh_, scalar2=None, op0=ALU.is_gt))
        W("vector", ("vv", rg_))
        r = OP("vector", "vv", lambda e, bb=b, d_=dst, hh_=hn: e.scalar_tensor_tensor(
            out=d_, in0=wst[bb][:].rearrange("p (s x) -> p s x", s=2),
            scalar=hh_, in1=d_, op0=ALU.is_lt, op1=ALU.subtract))
        u_free[b] = ("vv", r)
        pair_last[mat] = r
        if li == 31:
            sup_ready[2 * q] = sup_ready[2 * q + 1] = (
                ("vv", pair_last[0]), ("vv", pair_last[1]))

    def emit_one_down_tile(i, h_rdy_d):
        b = i % 2
        W("sync", d_free[b])
        state["dq"][b] = DMA("sync", f"s2g{b}", wstd[b][:],
                             wd_in[128 * i:128 * (i + 1), :])
        W("vector", (f"s2g{b}", state["dq"][b]), d_tq[b], h_rdy_d)
        rg_ = OP("vector", "vv", lambda e, bb=b: e.tensor_scalar(
            out=terd_t[bb][:], in0=wstd[bb][:], scalar1=h_d, scalar2=None,
            op0=ALU.is_gt))
        W("vector", ("vv", rg_))
        r = OP("vector", "vv", lambda e, bb=b: e.scalar_tensor_tensor(
            out=terd_t[bb][:], in0=wstd[bb][:], scalar=nh_d, in1=terd_t[bb][:],
            op0=ALU.is_lt, op1=ALU.subtract))
        d_free[b] = ("vv", r)
        W("gpsimd", ("vv", r))
        w = DMA("gpsimd", f"qw{b}", scr2_d[128 * i:128 * (i + 1), :], terd_t[b][:])
        d_tq[b] = (f"qw{b}", w)

    v_hid = [0] * (EC + 2)
    state_v = {"silu": 0, "flush": 0, "z2tok": 0}
    flush_v = [0] * DC
    dn_free = [("tv", 0)] * 2
    h_rdy_d_v = [("vv", 0)]

    def emit_A_block(b, first_waits, steps):
        si_extra = [0, 0]
        for si in range(2):
            s = 2 * b + si
            W("tensor", sup_ready[s][0], sup_ready[s][1],
              *(first_waits if si == 0 else []))
            for ci in range(SUPC):
                ec = SUPC * s + ci
                ecb = ec % BEC
                pa = ps_a[2 * (ec % 2)]
                pg = ps_a[2 * (ec % 2) + 1]
                if ec >= 2:
                    W("tensor", ("vv", v_hid[ec - 2]))
                v_mmu = v_mmg = 0
                for hh in range(2):
                    for c in range(DC):
                        fn = lambda e, p_=pa, cc=c, h=hh, s_=s, ci_=ci: e.matmul(
                            p_[:, 512 * h:512 * (h + 1)],
                            wupT[:, 8192 * (s_ % 2) + SUPE * cc + 128 * ci_:
                                 8192 * (s_ % 2) + SUPE * cc + 128 * (ci_ + 1)],
                            s_fm[:, T * cc + 512 * h:T * cc + 512 * (h + 1)],
                            start=(cc == 0), stop=(cc == 15))
                        if hh == 1 and c == 15:
                            v_mmu = OP("tensor", "tv", fn)
                        else:
                            OPq("tensor", fn)
                for hh in range(2):
                    for c in range(DC):
                        fn = lambda e, p_=pg, cc=c, h=hh, s_=s, ci_=ci: e.matmul(
                            p_[:, 512 * h:512 * (h + 1)],
                            wgT[:, 8192 * (s_ % 2) + SUPE * cc + 128 * ci_:
                                8192 * (s_ % 2) + SUPE * cc + 128 * (ci_ + 1)],
                            s_fm[:, T * cc + 512 * h:T * cc + 512 * (h + 1)],
                            start=(cc == 0), stop=(cc == 15))
                        if hh == 1 and c == 15:
                            v_mmg = OP("tensor", "tv", fn)
                        else:
                            OPq("tensor", fn)
                mm_sup_done[s] = v_mmg
                sb2 = ec % 2
                W("scalar", ("tv", v_mmu), h_rdy_u,
                  *( [("vv", v_hid[ec - 2])] if ec >= 2 else []))
                state_v["silu"] = OP("scalar", "av", lambda e, p_=pa, i=sb2: e.activation(
                    out=sil_t[i][:], in_=p_[:], func=AF.Silu, scale=nsu))
                W("vector", ("av", state_v["silu"]), ("tv", v_mmg), h_rdy_g)
                v_hid[ec] = OP("vector", "vv",
                               lambda e, p_=pg, i=sb2, eb=ecb, hb=b % 2:
                               e.scalar_tensor_tensor(
                                   out=hid[hb][:, T * eb:T * (eb + 1)], in0=p_[:],
                                   scalar=nsg, in1=sil_t[i][:],
                                   op0=ALU.mult, op1=ALU.mult))
                for _ in range(6):
                    if steps:
                        steps.pop(0)()
        while steps:
            steps.pop(0)()

    def emit_B_block(b, z2_parts, ug_steps):
        dnq = [0, 0]
        z2prev = ("vv", 0)
        for c in range(DC):
            g4 = c // 2
            db = g4 % 2
            pd = ps_a[2 * (c % 2)]
            if c % 2 == 0:
                W("sync", scr2_q[b][0], scr2_q[b][1], dn_free[db])
                dnq[db] = DMA("sync", f"wd{db}",
                              wdn[db][:].rearrange("p (k j) -> p k j", j=256),
                              scr2_d[1024 * b:1024 * (b + 1), 256 * g4:256 * (g4 + 1)]
                              .rearrange("(k p) j -> p k j", p=128))
            W("tensor", (f"wd{db}", dnq[db]), ("vv", v_hid[BEC * b + BEC - 1]))
            if c >= 2:
                W("tensor", ("vv", state_v["flush"]))
            else:
                W("tensor", ("av", state_v["silu"]))
            v_mmd = 0
            for hh in range(2):
                for k in range(BEC):
                    fn = lambda e, p_=pd, kk=k, h=hh, d_=db, j_=c % 2, hb=b % 2: e.matmul(
                        p_[:, 512 * h:512 * (h + 1)],
                        wdn[d_][:, 256 * kk + 128 * j_:256 * kk + 128 * (j_ + 1)],
                        hid[hb][:, T * kk + 512 * h:T * kk + 512 * (h + 1)],
                        start=(kk == 0), stop=(kk == BEC - 1))
                    if hh == 1 and k == BEC - 1:
                        v_mmd = OP("tensor", "tv", fn)
                    else:
                        OPq("tensor", fn)
            if c % 2 == 1:
                dn_free[db] = ("tv", v_mmd)
            if b == 0:
                W("vector", ("tv", v_mmd), h_rdy_d_v[0])
                flush_v[c] = OP("vector", "vv",
                                lambda e, p_=pd, cc=c: e.scalar_tensor_tensor(
                                    out=yac[:, T * cc:T * (cc + 1)], in0=p_[:],
                                    scalar=nsd, in1=s_fm[:, T * cc:T * (cc + 1)],
                                    op0=ALU.mult, op1=ALU.add))
                state_v["flush"] = flush_v[c]
                for _ in range(2):
                    if ug_steps:
                        ug_steps.pop(0)()
            elif b < NB - 1:
                W("vector", ("tv", v_mmd), ("vv", flush_v[c]))
                flush_v[c] = OP("vector", "vv",
                                lambda e, p_=pd, cc=c: e.scalar_tensor_tensor(
                                    out=yac[:, T * cc:T * (cc + 1)], in0=p_[:],
                                    scalar=nsd, in1=yac[:, T * cc:T * (cc + 1)],
                                    op0=ALU.mult, op1=ALU.add))
                state_v["flush"] = flush_v[c]
                for _ in range(2):
                    if ug_steps:
                        ug_steps.pop(0)()
            else:
                z2f, z2tok = z2_parts
                W("vector", ("tv", v_mmd), ("vv", flush_v[c]), z2prev)
                v_z2 = OP("vector", "vv", lambda e, p_=pd, cc=c: e.scalar_tensor_tensor(
                    out=z2f[:], in0=p_[:], scalar=nsd,
                    in1=yac[:, T * cc:T * (cc + 1)], op0=ALU.mult, op1=ALU.add))
                W("tensor", ("vv", v_z2))
                for half in range(2):
                    pt_ = ps_a[1 + 2 * half]
                    for j in range(4):
                        t_i = 4 * half + j
                        v_tp2 = OP("tensor", "tv", lambda e, p_=pt_, t=t_i: e.transpose(
                            p_[:, 128 * (t % 4):128 * (t % 4 + 1)],
                            z2f[:, 128 * t:128 * (t + 1)], ident[:]))
                    W("vector", ("tv", v_tp2))
                    vzt = OP("vector", "vv", lambda e, p_=pt_, cc=c, hf=half: e.tensor_copy(
                        out=z2tok[:].rearrange("p (t x) -> p t x", x=D)
                        [:, 4 * hf:4 * (hf + 1), 128 * cc:128 * (cc + 1)],
                        in_=p_[:, 0:512].rearrange("p (t x) -> p t x", x=128)))
                z2prev = ("vv", vzt)
                state_v["flush"] = vzt
                state_v["z2tok"] = vzt
                for _ in range(2):
                    if ug_steps:
                        ug_steps.pop(0)()
        while ug_steps:
            ug_steps.pop(0)()

    ptiles = [(0, cc, mat) for cc in range(DC) for mat in range(2)]
    for t6 in ptiles[:6]:
        emit_ug_load(*t6)
    for ki, t in enumerate(ptiles):
        emit_ug_quant(*t)
        if ki + 6 < 32:
            emit_ug_load(*ptiles[ki + 6])

    first_A = [("sfm", v_sfm), ("misc", v_ident)]
    z2_parts = None
    for b in range(NB):
        tiles = ([(b + 1, cc, mat) for cc in range(DC) for mat in range(2)]
                 if b + 1 < NB else [])
        steps = []
        for t6 in tiles[:6]:
            steps.append(lambda t=t6: emit_ug_load(*t))
        if b == 0:
            def fin2_step():
                h_rdy_d_v[0] = emit_fin(2, ("vv", dlast), acc_d, "gpsimd")
            steps.append(fin2_step)
            for m_ in range(3):
                for k_ in range(BEC):
                    steps.append(lambda i_=BEC * m_ + k_:
                                 emit_one_down_tile(i_, h_rdy_d_v[0]))
                steps.append(lambda m=m_: scr2_q.__setitem__(
                    m, (d_tq[0], d_tq[1])))
        if b >= 1 and b + 2 < NSLAB:
            for k in range(BEC):
                steps.append(lambda i_=BEC * (b + 2) + k:
                             emit_one_down_tile(i_, h_rdy_d_v[0]))
        ug_steps = []

        def mk_step(ki, t, tl):
            def f():
                emit_ug_quant(*t)
                if ki + 6 < len(tl):
                    emit_ug_load(*tl[ki + 6])
            return f
        for ki, t in enumerate(tiles):
            ug_steps.append(mk_step(ki, t, tiles))
        emit_A_block(b, first_A if b == 0 else
                     [("vv", state_v["flush"]), ("av", state_v["silu"])], steps)
        if b + 2 < NSLAB and b >= 1:
            scr2_q[b + 2] = (d_tq[0], d_tq[1])
        if b == NB - 1:
            supsc.close()
            z2f = FN(nc.sbuf_tensor("z2f", [128, T], F32))
            z2tok = FN(nc.sbuf_tensor("z2tok", [128, TT * D], BF16))
            ostage = FN(nc.sbuf_tensor("ost", [128, D], F32))
            z2_parts = (z2f, z2tok)
            emit_B_block(b, z2_parts, ug_steps)
        else:
            emit_B_block(b, None, ug_steps)

    W("vector", ("vv", state_v["z2tok"]))
    v_mv2 = 0
    aggr_v = [0, 0]
    for t in range(TT):
        sl = t % 2
        if aggr_v[sl]:
            W("vector", ("vv", aggr_v[sl]))
        v_bn2 = 0
        for j in range(4):
            v_bn2 = OP("vector", "vv", lambda e, t_=t, j_=j, s_=sl: e.bn_stats(
                out=bnst[:, 24 * s_ + 6 * j_:24 * s_ + 6 * (j_ + 1)],
                in_=z2tok[:, D * t_ + 512 * j_:D * t_ + 512 * (j_ + 1)]))
        W("vector", ("vv", v_bn2))
        v_mv2 = OP("vector", "vv", lambda e, t_=t, s_=sl: e.bn_aggr(
            out=mv[:, 2 * t_:2 * (t_ + 1)], in_=bnst[:, 24 * s_:24 * (s_ + 1)]))
        aggr_v[sl] = v_mv2
    W("vector", ("vv", v_mv2))
    v_ve2 = OP("vector", "vv", lambda e: e.tensor_scalar(
        out=sqv[:], in0=mv[:, 1:2 * TT:2], scalar1=EPS, scalar2=None, op0=ALU.add))
    W("scalar", ("vv", v_ve2))
    v_sq2 = OP("scalar", "av", lambda e: e.activation(
        out=sqv[:], in_=sqv[:], func=AF.Sqrt))
    W("vector", ("av", v_sq2))
    v_rs2 = OP("vector", "vv", lambda e: e.reciprocal(out=rstd[:], in_=sqv[:]))
    W("vector", ("vv", v_rs2))
    oq = 0
    for t in range(TT):
        W("vector", ("out0", oq))
        v_o = OP("vector", "vv", lambda e, t_=t: e.tensor_scalar(
            out=ostage[:], in0=z2tok[:, D * t_:D * (t_ + 1)],
            scalar1=mv[:, 2 * t_:2 * t_ + 1], scalar2=rstd[:, t_:t_ + 1],
            op0=ALU.subtract, op1=ALU.mult))
        W("sync", ("vv", v_o))
        oq = DMA("sync", "out0", out[128 * t:128 * (t + 1), :], ostage[:])
    W("sync", ("out0", oq))
    W("gpsimd", ("out0", oq))

    with nc.Block() as block:
        @block.sync
        def _(e):
            for f in ops["sync"]:
                f(e)

        @block.scalar
        def _(e):
            for f in ops["scalar"]:
                f(e)

        @block.vector
        def _(e):
            for f in ops["vector"]:
                f(e)

        @block.tensor
        def _(e):
            for f in ops["tensor"]:
                f(e)

        @block.gpsimd
        def _(e):
            for f in ops["gpsimd"]:
                f(e)

    ffn.close()
    es.close()
    return nc


_PT = np.zeros((8, 2, 256), np.float32)
for _lvl in range(8):
    _bits = (np.arange(256) >> (7 - _lvl)) & 1
    _PT[_lvl, 0, _bits == 0] = 1.0
    _PT[_lvl, 1, _bits == 1] = 1.0


def kernel(L0_attn_u, L0_attn_A_factors, L0_attn_W_b_rank, L0_attn_W_b_state,
           L0_attn_W_c_rank, L0_attn_W_c_model, L0_attn_D_vec,
           L0_ffn_W_up, L0_ffn_W_gate, L0_ffn_W_down):
    u = np.ascontiguousarray(np.asarray(L0_attn_u, np.float32).reshape(NCORES * T, D))
    at = np.ascontiguousarray(np.asarray(L0_attn_A_factors, np.float32).transpose(0, 2, 1))
    wbs_t = np.ascontiguousarray(np.asarray(L0_attn_W_b_state, np.float32).T)
    dvec = np.ascontiguousarray(np.asarray(L0_attn_D_vec, np.float32).reshape(DC, 128).T)

    nc = build_core()
    common = {
        "w_b_rank": np.ascontiguousarray(np.asarray(L0_attn_W_b_rank, np.float32)),
        "a_t": at, "pt_all": _PT, "w_bs_t": wbs_t,
        "w_c_rank": np.ascontiguousarray(np.asarray(L0_attn_W_c_rank, np.float32)),
        "w_c_model": np.ascontiguousarray(np.asarray(L0_attn_W_c_model, np.float32)),
        "d_vec": dvec,
        "w_up": np.ascontiguousarray(np.asarray(L0_ffn_W_up, np.float32).T),
        "w_gate": np.ascontiguousarray(np.asarray(L0_ffn_W_gate, np.float32).T),
        "w_down": np.ascontiguousarray(np.asarray(L0_ffn_W_down, np.float32).T),
        "ident": np.eye(128, dtype=np.float32),
    }
    in_maps = [dict(common, u=np.ascontiguousarray(u[T * i:T * (i + 1)]))
               for i in range(NCORES)]
    res = run_bass_kernel_spmd(nc, in_maps, core_ids=list(range(NCORES)))
    full = np.concatenate([np.asarray(r["out"]) for r in res.results], 0)
    return full.reshape(2, 4096, 2048).astype(np.float32)
